# version 04
# baseline (speedup 1.0000x reference)
import numpy as np
import ml_dtypes
import concourse.bass as bass
import concourse.mybir as mybir
from concourse.bass_utils import run_bass_kernel_spmd
from concourse.aot_env import allow_ici_all_to_all
from contextlib import ExitStack

F32 = mybir.dt.float32
BF16 = mybir.dt.bfloat16
AF = mybir.ActivationFunctionType
ALU = mybir.AluOpType
AX = mybir.AxisListType

NC = 8
D = 1024
DI = 2048
DS = 16
DTR = 64
FF = 4096
BLK = 512
NPH = 4
EPS = 1e-5
GC1 = 0.7978845608028654
GC3 = 0.044715

ENGINES = ("SP", "PE", "ACT", "DVE", "POOL")


class Plan:
    def __init__(self):
        self.entries = {e: [] for e in ENGINES}
        self.ccnt = {e: 0 for e in ENGINES}
        self.dcnt = {e: 0 for e in ENGINES}
        self.res = {}
        self.marks = {e: {} for e in ENGINES}
        self.n_coll = 0

    STRICT_SAME_ENGINE = True

    def _add(self, eng, dep, waits, is_dma):
        if dep is None:
            return
        sem, val = dep
        if (not is_dma) and sem == "c:" + eng and not Plan.STRICT_SAME_ENGINE:
            return
        if self.marks[eng].get(sem, 0) >= val:
            return
        if waits.get(sem, 0) < val:
            waits[sem] = val

    def global_barrier(self):
        vals = {}
        for e in ENGINES:
            if self.ccnt[e]:
                vals["c:" + e] = self.ccnt[e]
            if self.dcnt[e]:
                vals["d:" + e] = 16 * self.dcnt[e]
        for i in range(self.n_coll):
            vals["x:%d" % i] = 1
        for e in ENGINES:
            waits = [(s, v) for s, v in sorted(vals.items())
                     if s != "c:" + e and self.marks[e].get(s, 0) < v]
            for s, v in waits:
                self.marks[e][s] = v
            self.entries[e].append((waits, None, None))

    BARRIER_EVERY = 0
    _ec = 0

    def emit(self, eng, kind, fn, reads=(), writes=()):
        Plan._ec += 1
        if Plan.BARRIER_EVERY and Plan._ec % Plan.BARRIER_EVERY == 0:
            self.global_barrier()
        is_dma = kind in ("dma", "coll")
        waits = {}
        for k in reads:
            r = self.res.get(k)
            if r:
                self._add(eng, r["w"], waits, is_dma)
        for k in writes:
            r = self.res.get(k)
            if r:
                self._add(eng, r["w"], waits, is_dma)
                for d in r["r"]:
                    self._add(eng, d, waits, is_dma)
        for sem, val in waits.items():
            self.marks[eng][sem] = val
        if kind == "compute":
            self.ccnt[eng] += 1
            comp = ("c:" + eng, self.ccnt[eng])
            inc = ("c:" + eng, 1)
        elif kind == "dma":
            self.dcnt[eng] += 1
            comp = ("d:" + eng, 16 * self.dcnt[eng])
            inc = ("d:" + eng, 16)
        elif kind == "coll":
            comp = ("x:%d" % self.n_coll, 1)
            inc = ("x:%d" % self.n_coll, None)
            self.n_coll += 1
        else:
            comp, inc = None, None
        self.entries[eng].append((sorted(waits.items()), fn, inc))
        for k in writes:
            self.res[k] = {"w": comp, "r": []}
        if comp is not None:
            for k in reads:
                r = self.res.setdefault(k, {"w": None, "r": []})
                r["r"].append(comp)


def build_program():
    nc = bass.Bass("TRN2", target_bir_lowering=False, debug=False)
    P = Plan()

    def inp(name, shape, dtype=F32):
        return nc.dram_tensor(name, list(shape), dtype, kind="ExternalInput").ap()

    xb = inp("xb", [NPH, 3 + BLK, D])
    hmask = inp("hmask", [128, NPH])
    a_log_my = inp("a_log_my", [2, 128, DS])
    ln1g_b = inp("ln1g_b", [128, D])
    ln1b_b = inp("ln1b_b", [128, D])
    w_in_f = inp("w_in_f", [D, 2 * DI])
    conv_w_fm = inp("conv_w_fm", [128, 64])
    conv_b_fm = inp("conv_b_fm", [128, 16])
    w_x_fm = inp("w_x_fm", [128, 16 * 96])
    w_dt_f = inp("w_dt_f", [DTR, DI])
    b_dt_fm = inp("b_dt_fm", [128, 16])
    d_skip_fm = inp("d_skip_fm", [128, 16])
    w_out_f = inp("w_out_f", [DI, D])
    ln2g_fm = inp("ln2g_fm", [128, 8])
    ln2b_fm = inp("ln2b_fm", [128, 8])
    w_ff1_f = inp("w_ff1_f", [D, 2 * FF])
    b_ff1_fm = inp("b_ff1_fm", [128, 64])
    w_ff2_f = inp("w_ff2_f", [FF, D])
    b_ff2_fm = inp("b_ff2_fm", [128, 8])
    g1_in = inp("g1_in", [2, 32, 4, 32])
    g2_perm = inp("g2_perm", [2, 4, 16, 16])
    br_fm = inp("br_fm", [128, 4])
    bi_fm = inp("bi_fm", [128, 4])
    idf = inp("idf", [128, 128])
    idb = inp("idb", [128, 128], BF16)
    ones_r = inp("ones_r", [1, 128])
    half_r = inp("half_r", [1, 128])
    invd_c = inp("invd_c", [128, 1])
    consts_in = inp("consts_in", [128, 16])

    out_d = nc.dram_tensor("out", [NPH * BLK, D], F32, kind="ExternalOutput").ap()

    def idram(name, shape, dtype=F32):
        return nc.dram_tensor(name, list(shape), dtype, kind="Internal")

    w_in_bf = idram("w_in_bf", [D, 2 * DI], BF16)
    w_out_bf = idram("w_out_bf", [DI, D], BF16)
    w_ff1_bf = idram("w_ff1_bf", [D, 2 * FF], BF16)
    w_ff2_bf = idram("w_ff2_bf", [FF, D], BF16)
    a2a_dt_i = [idram(f"a2a_dt_i{c}", [DI, BLK]) for c in range(NPH)]
    a2a_dt_o = [idram(f"a2a_dt_o{c}", [DI, BLK]) for c in range(NPH)]
    a2a_dx_i = [idram(f"a2a_dx_i{c}", [DI, BLK]) for c in range(NPH)]
    a2a_dx_o = [idram(f"a2a_dx_o{c}", [DI, BLK]) for c in range(NPH)]
    a2a_y_i = [idram(f"a2a_y_i{c}", [DI, BLK]) for c in range(NPH)]
    a2a_y_o = [idram(f"a2a_y_o{c}", [DI, BLK]) for c in range(NPH)]
    bc_i = [idram(f"bc_i{c}", [32, BLK]) for c in range(NPH)]
    bc_o = [idram(f"bc_o{c}", [NC * 32, BLK]) for c in range(NPH)]
    x_sp = [idram(f"x_sp{c}", [D, BLK]) for c in range(NPH)]
    dxc_sp = [idram(f"dxc_sp{c}", [DI, BLK]) for c in range(NPH)]
    szu_sp = [idram(f"szu_sp{c}", [DI, BLK]) for c in range(NPH)]
    m_bounce = idram("m_bounce", [512, 512])
    fl_i = [idram(f"fl_i{c}", [1, 16]) for c in range(2 * NPH)]
    fl_o = [idram(f"fl_o{c}", [NC, 16]) for c in range(2 * NPH)]

    rg = [list(range(NC))]

    with ExitStack() as st:
        ec = st.enter_context

        def sbuf(name, shape, dtype=F32):
            return ec(nc.sbuf_tensor(name, list(shape), dtype))

        s_ln1g = sbuf("s_ln1g", [128, D])
        s_ln1b = sbuf("s_ln1b", [128, D])
        s_idf = sbuf("s_idf", [128, 128])
        s_idb = sbuf("s_idb", [128, 128], BF16)
        s_ones = sbuf("s_ones", [1, 128])
        s_half = sbuf("s_half", [1, 128])
        s_invd = sbuf("s_invd", [128, 1])
        s_const = sbuf("s_const", [128, 16])
        s_convw = sbuf("s_convw", [128, 64])
        s_convb = sbuf("s_convb", [128, 16])
        s_thb = sbuf("s_thb", [128, 16])
        s_bdt = sbuf("s_bdt", [128, 16])
        s_dsh = sbuf("s_dsh", [128, 16])
        s_wx = sbuf("s_wx", [128, 16 * 96])
        s_wdt = sbuf("s_wdt", [DTR, DI])
        s_alog = sbuf("s_alog", [128, 32])
        s_A = sbuf("s_A", [128, 32])
        s_carry = sbuf("s_carry", [128, 32])
        s_hm = sbuf("s_hm", [128, NPH])
        s_ln2g = sbuf("s_ln2g", [128, 8])
        s_ln2b = sbuf("s_ln2b", [128, 8])
        s_bff1 = sbuf("s_bff1", [128, 64])
        s_bff2 = sbuf("s_bff2", [128, 8])
        s_br = sbuf("s_br", [128, 4])
        s_bi = sbuf("s_bi", [128, 4])
        s_wrt = sbuf("s_wrt", [128, 4 * 512], BF16)
        s_wit = sbuf("s_wit", [128, 4 * 512], BF16)
        s_witn = sbuf("s_witn", [128, 4 * 512], BF16)

        A_N, A_W = 5, 515
        poolA = sbuf("poolA", [128, A_N * A_W])
        poolB = sbuf("poolB", [128, 16 * BLK])
        poolC = sbuf("poolC", [128, 16 * A_W], BF16)
        poolD = sbuf("poolD", [128, 32 * BLK], BF16)
        s_xfm = sbuf("s_xfm", [128, 8 * BLK])
        s_xt = sbuf("s_xt", [128, D])
        s_lnx = sbuf("s_lnx", [128, D])
        s_hx = sbuf("s_hx", [128, D], BF16)
        s_cols = sbuf("s_cols", [128, 16])
        s_dlo = sbuf("s_dlo", [DTR, BLK])
        s_bc32 = sbuf("s_bc32", [32, BLK])
        s_bcf = sbuf("s_bcf", [1, 8 * BLK])
        s_Bb = sbuf("s_Bb", [128, BLK])
        s_Cb = sbuf("s_Cb", [128, BLK])
        s_w1 = sbuf("s_w1", [128, 2 * BLK])
        s_row = sbuf("s_row", [1, 2 * BLK])
        pnl = sbuf("pnl", [128, 2 * 1024], BF16)
        stg_f = sbuf("stg_f", [128, 512])
        stg_b = sbuf("stg_b", [128, 512], BF16)
        s_mub = s_Bb
        s_rsb = s_Cb
        s_dseg = poolB[:, 0:2 * BLK]
        s_xseg = poolB[:, 2 * BLK:4 * BLK]
        s_ys = poolB[:, 4 * BLK:6 * BLK]
        s_hbuf = poolB[:, 6 * BLK:9 * BLK]

        p_big0 = ec(nc.psum_tensor("p_big0", [128, 512], F32))
        p_big1 = ec(nc.psum_tensor("p_big1", [128, 512], F32))
        p_aux0 = ec(nc.psum_tensor("p_aux0", [128, 512], F32))
        p_aux1 = ec(nc.psum_tensor("p_aux1", [128, 512], F32))
        p_trf = ec(nc.psum_tensor("p_trf", [128, 512], F32))
        p_trb = ec(nc.psum_tensor("p_trb", [128, 1024], BF16))
        p_stat = ec(nc.psum_tensor("p_stat", [1, 512], F32))

        sems = {}
        for e in ENGINES:
            sems["c:" + e] = ec(nc.semaphore("sc_" + e))
            sems["d:" + e] = ec(nc.semaphore("sd_" + e))
        for i in range(8 * NPH):
            sems["x:%d" % i] = ec(nc.semaphore("sx_%d" % i))

        block = ec(nc.Block())

        def dma(q, out, in_, reads=(), writes=()):
            P.emit(q, "dma", lambda e, o=out, i=in_: e.dma_start(out=o, in_=i),
                   reads=list(reads), writes=list(writes))

        def cop(eng, fn, reads=(), writes=()):
            P.emit(eng, "compute", fn, reads=list(reads), writes=list(writes))

        def coll(kind, ins, outs, reads=(), writes=()):
            def fn(e, k=kind, ii=ins, oo=outs):
                if k == "AllToAll":
                    with allow_ici_all_to_all():
                        return e.collective_compute(k, ALU.bypass, replica_groups=rg,
                                                    ins=ii, outs=oo)
                return e.collective_compute(k, ALU.bypass, replica_groups=rg,
                                            ins=ii, outs=oo)
            P.emit("POOL", "coll", fn, reads=list(reads), writes=list(writes))

        def pa(i, w=BLK):
            return poolA[:, i * A_W:i * A_W + w], f"pA{i}"

        def pb(i, w=BLK):
            return poolB[:, i * BLK:i * BLK + w], f"pB{i}"

        def pc(i, w=BLK):
            return poolC[:, i * A_W:i * A_W + w], f"pC{i}"

        def pd(i, w=BLK):
            return poolD[:, i * BLK:i * BLK + w], f"pD{i}"

        CONST_VALS = [EPS, 1.0 / D, 0.5, -0.5, -1.0, 1.0, GC3, GC1, 2.0, 0.0,
                      0.25, -2.0, 4.0, 0.125, 8.0, 3.0]
        for ci, cv in enumerate(CONST_VALS):
            nc.const_aps.register(F32, cv, s_const[:, ci:ci + 1])
        dma("SP", s_const[:], consts_in, writes=["k_const"])
        for _e in ENGINES:
            P.emit(_e, "fence", None, reads=["k_const"])
        for wf in (AF.Exp, AF.Ln, AF.Tanh, AF.Abs, AF.Relu, AF.Square, AF.Copy):
            cop("ACT", lambda e, f=wf: e.activation(s_cols[0:1, 5:6],
                                                    s_const[0:1, 5:6], f),
                reads=["k_const", "k_warm"], writes=["k_warm"])
        for _e in ENGINES:
            P.emit(_e, "fence", None, reads=["k_warm"])
        for (dst, src, key) in [
            (s_ln1g[:], ln1g_b, "k_ln1g"), (s_ln1b[:], ln1b_b, "k_ln1b"),
            (s_idf[:], idf, "k_idf"), (s_idb[:], idb, "k_idb"),
            (s_ones[:], ones_r, "k_ones"), (s_half[:], half_r, "k_half"),
            (s_invd[:], invd_c, "k_invd"),
            (s_convw[:], conv_w_fm, "k_convw"), (s_convb[:], conv_b_fm, "k_convb"),
            (s_bdt[:], b_dt_fm, "k_bdt"), (s_dsh[:], d_skip_fm, "k_dsh"),
            (s_wx[:], w_x_fm, "k_wx"), (s_wdt[:], w_dt_f, "k_wdt"),
            (s_hm[:], hmask, "k_hm"),
            (s_ln2g[:], ln2g_fm, "k_ln2g"), (s_ln2b[:], ln2b_fm, "k_ln2b"),
            (s_bff1[:], b_ff1_fm, "k_bff1"), (s_bff2[:], b_ff2_fm, "k_bff2"),
            (s_br[:], br_fm, "k_br"), (s_bi[:], bi_fm, "k_bi"),
        ]:
            dma("SP", dst, src, writes=[key])
        dma("SP", s_alog[:, 0:16], a_log_my[0], writes=["k_alog"])
        dma("SP", s_alog[:, 16:32], a_log_my[1], writes=["k_alog2"])

        cop("ACT", lambda e: e.activation(s_A[:], s_alog[:], AF.Exp),
            reads=["k_alog", "k_alog2"], writes=["k_A"])
        cop("DVE", lambda e: e.tensor_scalar_mul(s_A[:], s_A[:], -1.0),
            reads=["k_A"], writes=["k_A"])
        cop("DVE", lambda e: e.tensor_scalar_mul(s_thb[:], s_convb[:], 0.5),
            reads=["k_convb"], writes=["k_thb"])
        cop("DVE", lambda e: e.tensor_scalar_mul(s_dsh[:], s_dsh[:], 0.5),
            reads=["k_dsh"], writes=["k_dsh"])
        cop("DVE", lambda e: e.tensor_scalar_mul(s_wx[:], s_wx[:], 0.5),
            reads=["k_wx"], writes=["k_wx"])
        cop("DVE", lambda e: e.memset(s_carry[:], 0.0), writes=["k_carry"])

        def cast_weight(src_ap, dst_dram, rows, cols, key):
            for r in range(rows // 128):
                for cc in range(cols // 512):
                    fs, bs, kf, kb = stg_f[:], stg_b[:], "stgf", "stgb"
                    dma("SP", fs, src_ap[128 * r:128 * (r + 1), 512 * cc:512 * (cc + 1)],
                        reads=[kf], writes=[kf])
                    cop("POOL", lambda e, a=bs, b=fs: e.tensor_copy(a, b),
                        reads=[kf, kb], writes=[kb])
                    dma("ACT", dst_dram.ap()[128 * r:128 * (r + 1), 512 * cc:512 * (cc + 1)],
                        bs, reads=[kb], writes=[key])

        cast_weight(w_in_f, w_in_bf, D, 2 * DI, "k_winbf")
        cast_weight(w_out_f, w_out_bf, DI, D, "k_woutbf")
        cast_weight(w_ff1_f, w_ff1_bf, D, 2 * FF, "k_wff1bf")
        cast_weight(w_ff2_f, w_ff2_bf, FF, D, "k_wff2bf")

        pM = [p_big0, p_big1, p_aux0, p_aux1]
        pM_k = ["ps_b0", "ps_b1", "ps_a0", "ps_a1"]
        s_m0 = poolB[:, 0:2048]
        s_bda = poolB[:, 2048:4096]
        M0K = ["pB0", "pB1", "pB2", "pB3"]
        BDK = ["pB4", "pB5", "pB6", "pB7"]
        for ri in range(2):
            for c4 in range(4):
                cop("DVE", lambda e: e.memset(s_m0[:], 0.0),
                    reads=M0K, writes=M0K)
                for i32 in range(32):
                    tile, prt = divmod(16 * i32, 128)
                    dma("SP", s_m0[prt:prt + 16,
                                   tile * 512 + 16 * i32:tile * 512 + 16 * i32 + 16],
                        g2_perm[ri, c4], reads=M0K, writes=M0K)
                dma("SP", m_bounce.ap().rearrange("(t p) c -> p t c", p=128),
                    s_m0.rearrange("p (t c) -> p t c", c=512),
                    reads=M0K, writes=["k_mb"])
                prm = m_bounce.ap().rearrange("(d g) c -> g d c", d=32, g=16)
                for t4 in range(4):
                    dma("SP", s_m0[:, t4 * 512:(t4 + 1) * 512],
                        prm[4 * t4:4 * (t4 + 1), :, :],
                        reads=["k_mb"] + M0K, writes=M0K)
                cop("DVE", lambda e: e.memset(s_bda[:], 0.0),
                    reads=BDK, writes=BDK)
                for i16 in range(16):
                    tile, prt = divmod(32 * i16, 128)
                    dma("SP", s_bda[prt:prt + 32,
                                    tile * 512 + 32 * i16:tile * 512 + 32 * i16 + 32],
                        g1_in[ri, :, c4, :], reads=BDK, writes=BDK)
                for it in range(4):
                    for kt in range(4):
                        lh = s_bda[:, kt * 512 + 128 * it:kt * 512 + 128 * it + 128]
                        rh = s_m0[:, kt * 512:(kt + 1) * 512]
                        cop("PE", lambda e, o=pM[it][:, 0:512], a=lh, b=rh,
                            f=(c4 == 0 and kt == 0), l=(c4 == 3 and kt == 3):
                            e.matmul(o, a, b, start=f, stop=l),
                            reads=BDK + M0K, writes=[pM_k[it]])
            for it in range(4):
                cop("ACT", lambda e, o=s_m0[:, it * 512:(it + 1) * 512],
                    i=pM[it][:, 0:512]: e.activation(o, i, AF.Copy),
                    reads=[pM_k[it]] + M0K, writes=M0K)
            dma("SP", m_bounce.ap().rearrange("(t p) c -> p t c", p=128),
                s_m0.rearrange("p (t c) -> p t c", c=512),
                reads=M0K, writes=["k_mb"])
            prm2 = m_bounce.ap().rearrange("(g d) c -> d g c", g=16, d=32)
            for t4 in range(4):
                dma("SP", s_bda[:, t4 * 512:(t4 + 1) * 512],
                    prm2[8 * t4:8 * (t4 + 1), :, :],
                    reads=["k_mb"] + BDK, writes=BDK)
            dstb, kk = (s_wrt, "k_wrt") if ri == 0 else (s_wit, "k_wit")
            for t4 in range(4):
                cop("POOL", lambda e, o=dstb[:, t4 * 512:(t4 + 1) * 512],
                    i=s_bda[:, t4 * 512:(t4 + 1) * 512]: e.tensor_copy(o, i),
                    reads=BDK + [kk], writes=[kk])
            if ri == 1:
                cop("DVE", lambda e: e.tensor_scalar_mul(s_bda, s_bda, -1.0),
                    reads=BDK, writes=BDK)
                for t4 in range(4):
                    cop("POOL", lambda e, o=s_witn[:, t4 * 512:(t4 + 1) * 512],
                        i=s_bda[:, t4 * 512:(t4 + 1) * 512]: e.tensor_copy(o, i),
                        reads=BDK + ["k_witn"], writes=["k_witn"])

        P.global_barrier()
        P.global_barrier()

        for ph in range(NPH):
            for tt in range(5):
                rows = 128 if tt < 4 else 3
                xsl = s_xt[0:rows, :]
                xm = s_lnx[0:rows, :]
                hsl = s_hx[0:rows, :]
                kx, kxm, kh = "k_xt", "k_lnx", "k_hx"
                src = xb[ph, 3 + 128 * tt:3 + 128 * (tt + 1), :] if tt < 4 else xb[ph, 0:3, :]
                dma("SP", xsl, src, reads=[kx], writes=[kx])
                if tt < 4:
                    for half in range(2):
                        for dk4 in range(4):
                            dk = half * 4 + dk4
                            cop("PE", lambda e, o=p_trf[:, 128 * dk4:128 * (dk4 + 1)],
                                i=xsl[:, 128 * dk:128 * (dk + 1)], idn=s_idf[:]:
                                e.transpose(o, i, idn),
                                reads=[kx, "k_idf", "ps_tf"], writes=["ps_tf"])
                        for dk4 in range(4):
                            dk = half * 4 + dk4
                            cop("ACT", lambda e,
                                o=s_xfm[:, dk * BLK + 128 * tt:dk * BLK + 128 * (tt + 1)],
                                i=p_trf[:, 128 * dk4:128 * (dk4 + 1)]:
                                e.activation(o, i, AF.Copy),
                                reads=["ps_tf", f"k_xfm{dk}"], writes=[f"k_xfm{dk}"])
                sm = s_cols[0:rows, 0:1]
                mu = s_cols[0:rows, 1:2]
                vv = s_cols[0:rows, 2:3]
                r0 = s_cols[0:rows, 3:4]
                rs = s_cols[0:rows, 4:5]
                cop("ACT", lambda e, o=hsl, i=xsl, a=sm:
                    e.activation(o, i, AF.Copy, accum_out=a),
                    reads=[kx, kh, "k_c0"], writes=[kh, "k_c0"])
                cop("DVE", lambda e, o=mu, i=sm: e.tensor_scalar_mul(o, i, 1.0 / D),
                    reads=["k_c0", "k_c1"], writes=["k_c1"])
                cop("DVE", lambda e, o=xm, i=xsl, m=mu:
                    e.tensor_scalar(o, i, m, None, ALU.subtract),
                    reads=[kx, "k_c1", kxm], writes=[kxm])
                cop("ACT", lambda e, o=hsl, i=xm, a=vv:
                    e.activation(o, i, AF.Square, accum_out=a),
                    reads=[kxm, kh, "k_c2"], writes=[kh, "k_c2"])
                cop("ACT", lambda e, o=r0, i=vv:
                    e.activation(o, i, AF.Ln, bias=EPS, scale=1.0 / D),
                    reads=["k_c2", "k_c3"], writes=["k_c3"])
                cop("ACT", lambda e, o=rs, i=r0:
                    e.activation(o, i, AF.Exp, scale=-0.5),
                    reads=["k_c3", "k_c4"], writes=["k_c4"])
                cop("DVE", lambda e, o=xsl, i=xm, m=rs:
                    e.tensor_scalar(o, i, m, None, ALU.mult),
                    reads=[kxm, "k_c4", kx], writes=[kx])
                cop("DVE", lambda e, o=xm, a=xsl, g=s_ln1g[0:rows, :]:
                    e.tensor_mul(o, a, g),
                    reads=[kx, "k_ln1g", kxm], writes=[kxm])
                cop("DVE", lambda e, o=hsl, a=xm, b=s_ln1b[0:rows, :]:
                    e.tensor_add(o, a, b),
                    reads=[kxm, "k_ln1b", kh], writes=[kh])
                off = 3 + 128 * tt if tt < 4 else 0
                for dk in range(8):
                    cop("PE", lambda e, o=p_trb[:, 128 * dk:128 * dk + rows],
                        i=hsl[:, 128 * dk:128 * (dk + 1)], idn=s_idb[0:rows, 0:rows]:
                        e.transpose(o, i, idn),
                        reads=[kh, "k_idb", "ps_tb"], writes=["ps_tb"])
                for dk in range(8):
                    cop("ACT", lambda e,
                        o=poolC[:, dk * A_W + off:dk * A_W + off + rows],
                        i=p_trb[:, 128 * dk:128 * dk + rows]:
                        e.activation(o, i, AF.Copy),
                        reads=["ps_tb", f"pC{dk}"], writes=[f"pC{dk}"])
            xfm_keys = [f"k_xfm{dk}" for dk in range(8)]
            dma("ACT", x_sp[ph].ap().rearrange("(j p) t -> p j t", p=128),
                s_xfm[:].rearrange("p (j t) -> p j t", t=BLK),
                reads=xfm_keys, writes=["d_xsp"])

            hfm_keys = [f"pC{dk}" for dk in range(8)]

            for ot in range(32):
                pg, kpg = (p_big0, "ps_b0") if ot % 2 == 0 else (p_big1, "ps_b1")
                kpn = f"k_pnl{ot % 2}"
                pn = pnl[:, (ot % 2) * 1024:(ot % 2) * 1024 + 1024]
                dma("SP", pn.rearrange("p (kt c) -> p kt c", kt=8),
                    w_in_bf.ap().rearrange("(kt p) c -> p kt c", p=128)[:, :, 128 * ot:128 * (ot + 1)],
                    reads=["k_winbf", kpn], writes=[kpn])
                for kt in range(8):
                    cop("PE", lambda e, o=pg[:, 0:512], a=pn[:, kt * 128:(kt + 1) * 128],
                        b=poolC[:, kt * A_W + 3:kt * A_W + 515],
                        f=(kt == 0), l=(kt == 7):
                        e.matmul(o, a, b, start=f, stop=l),
                        reads=hfm_keys + [kpn], writes=[kpg])
                if ot < 16:
                    for kt in range(8):
                        cop("PE", lambda e, o=p_trf[:, 0:3], a=pn[:, kt * 128:(kt + 1) * 128],
                            b=poolC[:, kt * A_W:kt * A_W + 3],
                            f=(kt == 0), l=(kt == 7):
                            e.matmul(o, a, b, start=f, stop=l),
                            reads=hfm_keys + [kpn], writes=["ps_tf"])
                    j = ot
                    xzc, kxzc = pa(j % 2, 515)
                    cop("ACT", lambda e, o=xzc[:, 3:515], i=pg[:, 0:512]:
                        e.activation(o, i, AF.Copy),
                        reads=[kpg, kxzc], writes=[kxzc])
                    cop("ACT", lambda e, o=xzc[:, 0:3], i=p_trf[:, 0:3]:
                        e.activation(o, i, AF.Copy),
                        reads=["ps_tf", kxzc], writes=[kxzc])
                    cop("DVE", lambda e, o=xzc[:, 0:3], m=s_hm[:, ph:ph + 1]:
                        e.tensor_scalar(o, o, m, None, ALU.mult),
                        reads=["k_hm", kxzc], writes=[kxzc])
                    acc0 = s_w1[:, 0:BLK]
                    acc1 = s_w1[:, BLK:2 * BLK]
                    cop("DVE", lambda e, o=acc0, i=xzc[:, 0:512], w=s_convw[:, 4 * j:4 * j + 1]:
                        e.tensor_scalar(o, i, w, None, ALU.mult),
                        reads=[kxzc, "k_convw", "k_w10"], writes=["k_w10"])
                    cop("DVE", lambda e, o=acc1, i=xzc[:, 1:513],
                        w=s_convw[:, 4 * j + 1:4 * j + 2], a=acc0:
                        e.scalar_tensor_tensor(o, i, w, a, ALU.mult, ALU.add),
                        reads=[kxzc, "k_convw", "k_w10", "k_w11"], writes=["k_w11"])
                    cop("DVE", lambda e, o=acc0, i=xzc[:, 2:514],
                        w=s_convw[:, 4 * j + 2:4 * j + 3], a=acc1:
                        e.scalar_tensor_tensor(o, i, w, a, ALU.mult, ALU.add),
                        reads=[kxzc, "k_convw", "k_w11", "k_w10"], writes=["k_w10"])
                    cop("DVE", lambda e, o=acc1, i=xzc[:, 3:515],
                        w=s_convw[:, 4 * j + 3:4 * j + 4], a=acc0:
                        e.scalar_tensor_tensor(o, i, w, a, ALU.mult, ALU.add),
                        reads=[kxzc, "k_convw", "k_w10", "k_w11"], writes=["k_w11"])
                    th, kth = pa(2)
                    cop("ACT", lambda e, o=th, i=acc1, b=s_thb[:, j:j + 1]:
                        e.activation(o, i, AF.Tanh, scale=0.5, bias=b),
                        reads=["k_w11", "k_thb", kth], writes=[kth])
                    vt, kvt = pa(3)
                    cop("DVE", lambda e, o=vt, i=acc1, b=s_convb[:, j:j + 1], t=th:
                        e.scalar_tensor_tensor(o, i, b, t, ALU.add, ALU.mult),
                        reads=["k_w11", "k_convb", kth, kvt], writes=[kvt])
                    xcu, kxcu = pb(j)
                    cop("DVE", lambda e, o=xcu, i=acc1, b=s_convb[:, j:j + 1], t=vt:
                        e.scalar_tensor_tensor(o, i, b, t, ALU.add, ALU.add),
                        reads=["k_w11", "k_convb", kvt, kxcu], writes=[kxcu])
                    dx, kdx = pa(4)
                    cop("POOL", lambda e, o=dx, i=xcu, m=s_dsh[:, j:j + 1]:
                        e.tensor_scalar(o, i, m, None, ALU.mult),
                        reads=[kxcu, "k_dsh", kdx], writes=[kdx])
                    dma("ACT", dxc_sp[ph].ap()[128 * j:128 * (j + 1), :], dx,
                        reads=[kdx], writes=["d_dxc"])
                else:
                    jz = ot - 16
                    th, kth = pa(2)
                    cop("ACT", lambda e, o=th, i=pg[:, 0:512]:
                        e.activation(o, i, AF.Tanh, scale=0.5),
                        reads=[kpg, kth], writes=[kth])
                    t1, kt1 = pa(3)
                    cop("DVE", lambda e, o=t1, a=pg[:, 0:512], b=th: e.tensor_mul(o, a, b),
                        reads=[kpg, kth, kt1], writes=[kt1])
                    sz, ksz = pa(4)
                    cop("DVE", lambda e, o=sz, a=pg[:, 0:512], b=t1: e.tensor_add(o, a, b),
                        reads=[kpg, kt1, ksz], writes=[ksz])
                    dma("ACT", szu_sp[ph].ap()[128 * jz:128 * (jz + 1), :], sz,
                        reads=[ksz], writes=["d_szu"])

            xcu_keys = [f"pB{j}" for j in range(16)]

            for j in range(16):
                cop("PE", lambda e, o=p_aux0[0:64, 0:512],
                    a=s_wx[:, 96 * j:96 * j + 64], b=pb(j)[0],
                    f=(j == 0), l=(j == 15):
                    e.matmul(o, a, b, start=f, stop=l),
                    reads=xcu_keys + ["k_wx"], writes=["ps_a0"])
            for j in range(16):
                cop("PE", lambda e, o=p_aux1[0:32, 0:512],
                    a=s_wx[:, 96 * j + 64:96 * (j + 1)], b=pb(j)[0],
                    f=(j == 0), l=(j == 15):
                    e.matmul(o, a, b, start=f, stop=l),
                    reads=xcu_keys + ["k_wx"], writes=["ps_a1"])
            cop("ACT", lambda e: e.activation(s_dlo[:], p_aux0[0:64, 0:512], AF.Copy),
                reads=["ps_a0", "k_dlo"], writes=["k_dlo"])
            cop("ACT", lambda e: e.activation(s_bc32[:], p_aux1[0:32, 0:512], AF.Copy),
                reads=["ps_a1", "k_bc32"], writes=["k_bc32"])
            dma("ACT", bc_i[ph].ap(), s_bc32[:], reads=["k_bc32"], writes=["d_bci"])

            for j in range(16):
                pg, kpg = (p_big0, "ps_b0") if j % 2 == 0 else (p_big1, "ps_b1")
                cop("PE", lambda e, o=pg[:, 0:512], a=s_wdt[:, 128 * j:128 * (j + 1)],
                    b=s_dlo[:]:
                    e.matmul(o, a, b, start=True, stop=True),
                    reads=["k_dlo", "k_wdt"], writes=[kpg])
                ab = s_w1[:, 0:BLK]
                ee = s_w1[:, BLK:2 * BLK]
                cop("ACT", lambda e, o=ab, i=pg[:, 0:512], b=s_bdt[:, j:j + 1]:
                    e.activation(o, i, AF.Abs, bias=b),
                    reads=[kpg, "k_bdt", "k_w10"], writes=["k_w10"])
                cop("ACT", lambda e, o=ee, i=ab: e.activation(o, i, AF.Exp, scale=-1.0),
                    reads=["k_w10", "k_w11"], writes=["k_w11"])
                cop("ACT", lambda e, o=ab, i=ee: e.activation(o, i, AF.Ln, bias=1.0),
                    reads=["k_w11", "k_w10"], writes=["k_w10"])
                rl, krl = pa(2)
                cop("ACT", lambda e, o=rl, i=pg[:, 0:512], b=s_bdt[:, j:j + 1]:
                    e.activation(o, i, AF.Relu, bias=b),
                    reads=[kpg, "k_bdt", krl], writes=[krl])
                dtt, kdtt = pa(3)
                cop("DVE", lambda e, o=dtt, a=ab, b=rl: e.tensor_add(o, a, b),
                    reads=["k_w10", krl, kdtt], writes=[kdtt])
                dma("SP", a2a_dt_i[ph].ap()[128 * j:128 * (j + 1), :], dtt,
                    reads=[kdtt], writes=["d_dti"])
                dxt, kdxt = pa(4)
                cop("DVE", lambda e, o=dxt, a=dtt, b=pb(j)[0]: e.tensor_mul(o, a, b),
                    reads=[kdtt, f"pB{j}", kdxt], writes=[kdxt])
                dma("SP", a2a_dx_i[ph].ap()[128 * j:128 * (j + 1), :], dxt,
                    reads=[kdxt], writes=["d_dxi"])

            coll("AllToAll", [a2a_dt_i[ph].ap()], [a2a_dt_o[ph].ap()],
                 reads=["d_dti"], writes=[f"c_dt{ph}"])
            coll("AllToAll", [a2a_dx_i[ph].ap()], [a2a_dx_o[ph].ap()],
                 reads=["d_dxi"], writes=[f"c_dx{ph}"])
            coll("AllGather", [bc_i[ph].ap()], [bc_o[ph].ap()],
                 reads=["d_bci"], writes=[f"c_bc{ph}"])
            coll("AllGather", [fl_i[2 * ph].ap()], [fl_o[2 * ph].ap()],
                 reads=[f"c_dt{ph}", f"c_dx{ph}", f"c_bc{ph}"],
                 writes=[f"c_fl{ph}"])

            QSUB = 2048
            import os as _os
            _segbar = bool(_os.environ.get("KSEG_BARRIER"))
            for s8 in range(8):
                if _segbar:
                    P.global_barrier()
                kd = ["pB0", "pB1"]
                kx2 = ["pB2", "pB3"]
                dma("SP", s_dseg[:].rearrange("p (u t) -> p u t", u=2),
                    a2a_dt_o[ph].ap().rearrange("(u p) t -> p u t", p=128)[:, 2 * s8:2 * s8 + 2, :],
                    reads=[f"c_dt{ph}", f"c_fl{ph}"] + kd, writes=kd)
                dma("SP", s_xseg[:].rearrange("p (u t) -> p u t", u=2),
                    a2a_dx_o[ph].ap().rearrange("(u p) t -> p u t", p=128)[:, 2 * s8:2 * s8 + 2, :],
                    reads=[f"c_dx{ph}", f"c_fl{ph}"] + kx2, writes=kx2)
                for ng in range(4):
                    dma("SP", s_bcf[0:1, 0:4 * BLK],
                        bc_o[ph].ap()[32 * s8 + 4 * ng:32 * s8 + 4 * ng + 4, :],
                        reads=[f"c_bc{ph}", f"c_fl{ph}", "k_bcf"], writes=["k_bcf"])
                    dma("SP", s_bcf[0:1, 4 * BLK:8 * BLK],
                        bc_o[ph].ap()[32 * s8 + 16 + 4 * ng:32 * s8 + 16 + 4 * ng + 4, :],
                        reads=[f"c_bc{ph}", f"c_fl{ph}", "k_bcf"], writes=["k_bcf"])
                    for n4 in range(4):
                        n = 4 * ng + n4
                        kB, kC = "k_Bb", "k_Cb"
                        cop("PE", lambda e, o=p_aux0[:, 0:512], a=s_half[:],
                            b=s_bcf[0:1, n4 * BLK:(n4 + 1) * BLK]:
                            e.matmul(o, a, b, start=True, stop=True),
                            reads=["k_bcf", "k_half"], writes=["ps_a0"])
                        cop("ACT", lambda e, o=s_Bb[:], i=p_aux0[:, 0:512]:
                            e.activation(o, i, AF.Copy),
                            reads=["ps_a0", kB], writes=[kB])
                        cop("PE", lambda e, o=p_aux1[:, 0:512], a=s_ones[:],
                            b=s_bcf[0:1, (4 + n4) * BLK:(5 + n4) * BLK]:
                            e.matmul(o, a, b, start=True, stop=True),
                            reads=["k_bcf", "k_ones"], writes=["ps_a1"])
                        cop("ACT", lambda e, o=s_Cb[:], i=p_aux1[:, 0:512]:
                            e.activation(o, i, AF.Copy),
                            reads=["ps_a1", kC], writes=[kC])
                        for sub in range(2):
                            da = s_lnx[:, sub * BLK:(sub + 1) * BLK]
                            kda = "k_lnx"
                            cop("ACT", lambda e, o=da,
                                i=s_dseg[:, sub * BLK:(sub + 1) * BLK],
                                m=s_A[:, sub * 16 + n:sub * 16 + n + 1]:
                                e.activation(o, i, AF.Exp, scale=m),
                                reads=kd + ["k_A", kda], writes=[kda])
                            ub = s_xt[:, sub * BLK:(sub + 1) * BLK]
                            kub = "k_xt"
                            cop("DVE", lambda e, o=ub,
                                a=s_xseg[:, sub * BLK:(sub + 1) * BLK], b=s_Bb[:]:
                                e.tensor_mul(o, a, b),
                                reads=kx2 + [kB, kub], writes=[kub])
                            hslot = (2 * n + sub) % 3
                            hb = s_hbuf[:, hslot * BLK:(hslot + 1) * BLK]
                            khb = f"pB{6 + hslot}"
                            ckey = f"k_cr{sub}_{n}"
                            cop("DVE", lambda e, o=hb, a=da, b=ub,
                                i=s_carry[:, sub * 16 + n:sub * 16 + n + 1]:
                                e.tensor_tensor_scan(o, a, b, i, ALU.mult, ALU.add),
                                reads=[kda, kub, ckey, "k_carry", khb], writes=[khb])
                            cop("DVE", lambda e,
                                o=s_carry[:, sub * 16 + n:sub * 16 + n + 1],
                                i=hb[:, BLK - 1:BLK]: e.tensor_copy(o, i),
                                reads=[khb], writes=[ckey])
                            qkeys = [f"k_xfm{4 * sub + i}" for i in range(4)]
                            qv = s_xfm[:, sub * QSUB:(sub + 1) * QSUB] \
                                .rearrange("p (t k) -> p t k", k=4)[:, :, n4]
                            cop("DVE", lambda e, o=qv, a=hb, b=s_Cb[:]:
                                e.tensor_mul(o, a, b),
                                reads=[khb, kC] + qkeys, writes=qkeys)
                    for sub in range(2):
                        ysub = s_ys[:, sub * BLK:(sub + 1) * BLK]
                        kys = f"pB{4 + sub}"
                        qkeys = [f"k_xfm{4 * sub + i}" for i in range(4)]
                        qv = s_xfm[:, sub * QSUB:(sub + 1) * QSUB] \
                            .rearrange("p (t k) -> p t k", k=4)
                        if ng == 0:
                            cop("DVE", lambda e, o=ysub, i=qv:
                                e.tensor_reduce(o, i, AX.X, ALU.add),
                                reads=qkeys + [kys], writes=[kys])
                        else:
                            cop("DVE", lambda e, o=s_w1[:, 0:BLK], i=qv:
                                e.tensor_reduce(o, i, AX.X, ALU.add),
                                reads=qkeys + ["k_w10"], writes=["k_w10"])
                            cop("DVE", lambda e, o=ysub, a=ysub, b=s_w1[:, 0:BLK]:
                                e.tensor_add(o, a, b),
                                reads=["k_w10", kys], writes=[kys])
                for sub in range(2):
                    dma("SP",
                        a2a_y_i[ph].ap()[256 * s8 + 128 * sub:256 * s8 + 128 * (sub + 1), :],
                        s_ys[:, sub * BLK:(sub + 1) * BLK],
                        reads=[f"pB{4 + sub}"], writes=["d_yi"])
            coll("AllToAll", [a2a_y_i[ph].ap()], [a2a_y_o[ph].ap()],
                 reads=["d_yi"], writes=[f"c_y{ph}"])
            coll("AllGather", [fl_i[2 * ph + 1].ap()], [fl_o[2 * ph + 1].ap()],
                 reads=[f"c_y{ph}"], writes=[f"c_yfl{ph}"])

            dma("SP", s_xfm[:].rearrange("p (j t) -> p j t", j=8),
                x_sp[ph].ap().rearrange("(j p) t -> p j t", p=128),
                reads=["d_xsp"] + xfm_keys, writes=xfm_keys)

            for j in range(16):
                yt, kyt = pa(j % 2)
                dma("SP", yt, a2a_y_o[ph].ap()[128 * j:128 * (j + 1), :],
                    reads=[f"c_y{ph}", f"c_yfl{ph}", kyt], writes=[kyt])
                dx, kdx = pa(2)
                dma("SP", dx, dxc_sp[ph].ap()[128 * j:128 * (j + 1), :],
                    reads=["d_dxc", kdx], writes=[kdx])
                sz, ksz = pa(3)
                dma("SP", sz, szu_sp[ph].ap()[128 * j:128 * (j + 1), :],
                    reads=["d_szu", ksz], writes=[ksz])
                tt1 = s_w1[:, 0:BLK]
                cop("DVE", lambda e, o=tt1, a=yt, b=dx: e.tensor_add(o, a, b),
                    reads=[kyt, kdx, "k_w10"], writes=["k_w10"])
                yd, kyd = pc(j)
                cop("DVE", lambda e, o=yd, i=tt1, z=sz:
                    e.scalar_tensor_tensor(o, i, 0.5, z, ALU.mult, ALU.mult),
                    reads=["k_w10", ksz, kyd], writes=[kyd])
            ydot_keys = [f"pC{j}" for j in range(16)]

            for ot in range(8):
                pg, kpg = (p_big0, "ps_b0") if ot % 2 == 0 else (p_big1, "ps_b1")
                for grp in range(2):
                    kpn = f"k_pnl{grp}"
                    pn = pnl[:, grp * 1024:(grp + 1) * 1024]
                    dma("SP", pn.rearrange("p (kt c) -> p kt c", kt=8),
                        w_out_bf.ap().rearrange("(kt p) c -> p kt c", p=128)
                        [:, 8 * grp:8 * (grp + 1), 128 * ot:128 * (ot + 1)],
                        reads=["k_woutbf", kpn], writes=[kpn])
                    for kt in range(8):
                        cop("PE", lambda e, o=pg[:, 0:512], a=pn[:, kt * 128:(kt + 1) * 128],
                            b=pc(8 * grp + kt)[0],
                            f=(grp == 0 and kt == 0), l=(grp == 1 and kt == 7):
                            e.matmul(o, a, b, start=f, stop=l),
                            reads=ydot_keys + [kpn], writes=[kpg])
                xa, kxa = pb(ot)
                cop("DVE", lambda e, o=xa, i=pg[:, 0:512],
                    x=s_xfm[:, ot * BLK:(ot + 1) * BLK]:
                    e.scalar_tensor_tensor(o, i, 1.0, x, ALU.mult, ALU.add),
                    reads=[kpg, f"k_xfm{ot}", kxa], writes=[kxa])

            def fm_ln(src0, dst0, affine):
                skeys = [f"pB{src0 + j}" for j in range(8)]
                dkeys = [f"pB{dst0 + j}" for j in range(8)]
                for j in range(8):
                    cop("PE", lambda e, o=p_stat[0:1, 0:512], a=s_invd[:, 0:1],
                        b=pb(src0 + j)[0], f=(j == 0), l=(j == 7):
                        e.matmul(o, a, b, start=f, stop=l),
                        reads=[skeys[j], "k_invd"], writes=["ps_st"])
                cop("ACT", lambda e: e.activation(s_row[0:1, 0:BLK], p_stat[0:1, 0:512], AF.Copy),
                    reads=["ps_st", "k_r0"], writes=["k_r0"])
                cop("PE", lambda e: e.matmul(p_aux0[:, 0:512], s_ones[:],
                                             s_row[0:1, 0:BLK], start=True, stop=True),
                    reads=["k_r0", "k_ones"], writes=["ps_a0"])
                cop("ACT", lambda e: e.activation(s_mub[:], p_aux0[:, 0:512], AF.Copy),
                    reads=["ps_a0", "k_Bb"], writes=["k_Bb"])
                for j in range(8):
                    cop("DVE", lambda e, o=pb(dst0 + j)[0], a=pb(src0 + j)[0], b=s_mub[:]:
                        e.tensor_sub(o, a, b),
                        reads=[skeys[j], "k_Bb", dkeys[j]], writes=[dkeys[j]])
                for j in range(8):
                    sq2, ksq2 = pa(j % 2)
                    cop("ACT", lambda e, o=sq2, i=pb(dst0 + j)[0]:
                        e.activation(o, i, AF.Square),
                        reads=[dkeys[j], ksq2], writes=[ksq2])
                    cop("PE", lambda e, o=p_stat[0:1, 0:512], a=s_invd[:, 0:1], b=sq2,
                        f=(j == 0), l=(j == 7):
                        e.matmul(o, a, b, start=f, stop=l),
                        reads=[ksq2, "k_invd"], writes=["ps_st"])
                cop("ACT", lambda e: e.activation(s_row[0:1, BLK:2 * BLK],
                                                  p_stat[0:1, 0:512], AF.Ln, bias=EPS),
                    reads=["ps_st", "k_r1"], writes=["k_r1"])
                cop("ACT", lambda e: e.activation(s_row[0:1, BLK:2 * BLK],
                                                  s_row[0:1, BLK:2 * BLK], AF.Exp, scale=-0.5),
                    reads=["k_r1"], writes=["k_r1"])
                cop("PE", lambda e: e.matmul(p_aux1[:, 0:512], s_ones[:],
                                             s_row[0:1, BLK:2 * BLK], start=True, stop=True),
                    reads=["k_r1", "k_ones"], writes=["ps_a1"])
                cop("ACT", lambda e: e.activation(s_rsb[:], p_aux1[:, 0:512], AF.Copy),
                    reads=["ps_a1", "k_Cb"], writes=["k_Cb"])
                for j in range(8):
                    cop("DVE", lambda e, o=pb(dst0 + j)[0], a=pb(dst0 + j)[0], b=s_rsb[:]:
                        e.tensor_mul(o, a, b),
                        reads=["k_Cb", dkeys[j]], writes=[dkeys[j]])
                if affine is not None:
                    g_t, b_t, out_fn = affine
                    for j in range(8):
                        o_ap, o_k = out_fn(j)
                        cop("DVE", lambda e, o=o_ap, i=pb(dst0 + j)[0],
                            g=g_t[:, j:j + 1], b2=b_t[:, j:j + 1]:
                            e.tensor_scalar(o, i, g, b2, ALU.mult, ALU.add),
                            reads=[dkeys[j], "k_ln2g", "k_ln2b", o_k], writes=[o_k])

            fm_ln(0, 8, (s_ln2g, s_ln2b, lambda j: pc(j)))
            h2_keys = [f"pC{j}" for j in range(8)]

            for i in range(32):
                kpnA, kpnG = "k_pnl0", "k_pnl1"
                pnA = pnl[:, 0:1024]
                pnG = pnl[:, 1024:2048]
                dma("SP", pnA.rearrange("p (kt c) -> p kt c", kt=8),
                    w_ff1_bf.ap().rearrange("(kt p) c -> p kt c", p=128)[:, :, 128 * i:128 * (i + 1)],
                    reads=["k_wff1bf", kpnA], writes=[kpnA])
                dma("SP", pnG.rearrange("p (kt c) -> p kt c", kt=8),
                    w_ff1_bf.ap().rearrange("(kt p) c -> p kt c", p=128)[:, :, FF + 128 * i:FF + 128 * (i + 1)],
                    reads=["k_wff1bf", kpnG], writes=[kpnG])
                for kt in range(8):
                    cop("PE", lambda e, o=p_big0[:, 0:512], a=pnA[:, kt * 128:(kt + 1) * 128],
                        b=pc(kt)[0], f=(kt == 0), l=(kt == 7):
                        e.matmul(o, a, b, start=f, stop=l),
                        reads=h2_keys + [kpnA], writes=["ps_b0"])
                for kt in range(8):
                    cop("PE", lambda e, o=p_big1[:, 0:512], a=pnG[:, kt * 128:(kt + 1) * 128],
                        b=pc(kt)[0], f=(kt == 0), l=(kt == 7):
                        e.matmul(o, a, b, start=f, stop=l),
                        reads=h2_keys + [kpnG], writes=["ps_b1"])
                sqg, ksqg = pa(0)
                vv2, kvv2 = pa(1)
                tt2, ktt2 = pa(2)
                ga, kga = pa(3)
                gg2, kgg2 = pa(4)
                cop("DVE", lambda e, o=ga, i=p_big0[:, 0:512], b=s_bff1[:, i:i + 1]:
                    e.tensor_scalar(o, i, b, None, ALU.add),
                    reads=["ps_b0", "k_bff1", kga], writes=[kga])
                cop("DVE", lambda e, o=gg2, i=p_big1[:, 0:512], b=s_bff1[:, 32 + i:33 + i]:
                    e.tensor_scalar(o, i, b, None, ALU.add),
                    reads=["ps_b1", "k_bff1", kgg2], writes=[kgg2])
                cop("ACT", lambda e, o=sqg, i=gg2: e.activation(o, i, AF.Square),
                    reads=[kgg2, ksqg], writes=[ksqg])
                cop("POOL", lambda e, o=vv2, i=sqg:
                    e.tensor_scalar(o, i, GC3, 1.0, ALU.mult, ALU.add),
                    reads=[ksqg, kvv2], writes=[kvv2])
                cop("DVE", lambda e, o=tt2, a=vv2, b=gg2: e.tensor_mul(o, a, b),
                    reads=[kvv2, kgg2, ktt2], writes=[ktt2])
                cop("ACT", lambda e, o=tt2, i=tt2: e.activation(o, i, AF.Tanh, scale=GC1),
                    reads=[ktt2], writes=[ktt2])
                cop("DVE", lambda e, o=sqg, i=gg2, t=tt2:
                    e.scalar_tensor_tensor(o, i, 0.5, t, ALU.mult, ALU.mult),
                    reads=[kgg2, ktt2, ksqg], writes=[ksqg])
                cop("DVE", lambda e, o=vv2, i=gg2, t=sqg:
                    e.scalar_tensor_tensor(o, i, 0.5, t, ALU.mult, ALU.add),
                    reads=[kgg2, ksqg, kvv2], writes=[kvv2])
                fi, kfi = pd(i)
                cop("DVE", lambda e, o=fi, a=vv2, b=ga: e.tensor_mul(o, a, b),
                    reads=[kvv2, kga, kfi], writes=[kfi])
            ffin_keys = [f"pD{i}" for i in range(32)]

            for ot in range(8):
                pg, kpg = (p_big0, "ps_b0") if ot % 2 == 0 else (p_big1, "ps_b1")
                for hf in range(4):
                    kpn = f"k_pnl{hf % 2}"
                    pn = pnl[:, (hf % 2) * 1024:(hf % 2 + 1) * 1024]
                    dma("SP", pn.rearrange("p (kt c) -> p kt c", kt=8),
                        w_ff2_bf.ap().rearrange("(kt p) c -> p kt c", p=128)
                        [:, 8 * hf:8 * (hf + 1), 128 * ot:128 * (ot + 1)],
                        reads=["k_wff2bf", kpn], writes=[kpn])
                    for kt in range(8):
                        cop("PE", lambda e, o=pg[:, 0:512], a=pn[:, kt * 128:(kt + 1) * 128],
                            b=pd(8 * hf + kt)[0],
                            f=(hf == 0 and kt == 0), l=(hf == 3 and kt == 7):
                            e.matmul(o, a, b, start=f, stop=l),
                            reads=ffin_keys + [kpn], writes=[kpg])
                xb_, kxb = pb(8 + ot)
                cop("DVE", lambda e, o=xb_, i=pg[:, 0:512], b=s_bff2[:, ot:ot + 1],
                    x=pb(ot)[0]:
                    e.scalar_tensor_tensor(o, i, b, x, ALU.add, ALU.add),
                    reads=[kpg, "k_bff2", f"pB{ot}", kxb], writes=[kxb])

            for j in range(8):
                s1, ks1 = pb(j)
                cop("DVE", lambda e, o=s1, a=pb(8 + j)[0],
                    b=s_xfm[:, j * BLK:(j + 1) * BLK]:
                    e.tensor_add(o, a, b),
                    reads=[f"pB{8 + j}", f"k_xfm{j}", ks1], writes=[ks1])
            fm_ln(0, 8, None)
            for j in range(8):
                xcb, kxcb = pc(8 + j)
                cop("POOL", lambda e, o=xcb, i=pb(8 + j)[0]: e.tensor_copy(o, i),
                    reads=[f"pB{8 + j}", kxcb], writes=[kxcb])
            xcb_keys = [f"pC{8 + j}" for j in range(8)]

            for q in range(4):
                pg, kpg = (p_big0, "ps_b0") if q % 2 == 0 else (p_big1, "ps_b1")
                for kt in range(4):
                    cop("PE", lambda e, o=pg[:, 0:512],
                        a=s_wrt[:, kt * 512 + 128 * q:kt * 512 + 128 * q + 128],
                        b=pc(8 + kt)[0], f=(kt == 0), l=False:
                        e.matmul(o, a, b, start=f, stop=l),
                        reads=xcb_keys + ["k_wrt"], writes=[kpg])
                for kt in range(4):
                    cop("PE", lambda e, o=pg[:, 0:512],
                        a=s_witn[:, kt * 512 + 128 * q:kt * 512 + 128 * q + 128],
                        b=pc(12 + kt)[0], f=False, l=(kt == 3):
                        e.matmul(o, a, b, start=f, stop=l),
                        reads=xcb_keys + ["k_witn"], writes=[kpg])
                s2, ks2 = pb(q)
                cop("DVE", lambda e, o=s2, i=pg[:, 0:512], b=s_br[:, q:q + 1],
                    x=pb(8 + q)[0]:
                    e.scalar_tensor_tensor(o, i, b, x, ALU.add, ALU.add),
                    reads=[kpg, "k_br", f"pB{8 + q}", ks2], writes=[ks2])
            for q in range(4):
                pg, kpg = (p_big0, "ps_b0") if q % 2 == 0 else (p_big1, "ps_b1")
                for kt in range(4):
                    cop("PE", lambda e, o=pg[:, 0:512],
                        a=s_wit[:, kt * 512 + 128 * q:kt * 512 + 128 * q + 128],
                        b=pc(8 + kt)[0], f=(kt == 0), l=False:
                        e.matmul(o, a, b, start=f, stop=l),
                        reads=xcb_keys + ["k_wit"], writes=[kpg])
                for kt in range(4):
                    cop("PE", lambda e, o=pg[:, 0:512],
                        a=s_wrt[:, kt * 512 + 128 * q:kt * 512 + 128 * q + 128],
                        b=pc(12 + kt)[0], f=False, l=(kt == 3):
                        e.matmul(o, a, b, start=f, stop=l),
                        reads=xcb_keys + ["k_wrt"], writes=[kpg])
                s2, ks2 = pb(4 + q)
                cop("DVE", lambda e, o=s2, i=pg[:, 0:512], b=s_bi[:, q:q + 1],
                    x=pb(12 + q)[0]:
                    e.scalar_tensor_tensor(o, i, b, x, ALU.add, ALU.add),
                    reads=[kpg, "k_bi", f"pB{12 + q}", ks2], writes=[ks2])

            fm_ln(0, 8, None)

            for tk in range(4):
                for half in range(2):
                    for jj4 in range(4):
                        jj = half * 4 + jj4
                        cop("PE", lambda e, o=p_trf[:, 128 * jj4:128 * (jj4 + 1)],
                            i=pb(8 + jj)[0][:, 128 * tk:128 * (tk + 1)], idn=s_idf[:]:
                            e.transpose(o, i, idn),
                            reads=[f"pB{8 + jj}", "k_idf", "ps_tf"], writes=["ps_tf"])
                    for jj4 in range(4):
                        jj = half * 4 + jj4
                        cop("ACT", lambda e, o=s_xt[:, 128 * jj:128 * (jj + 1)],
                            i=p_trf[:, 128 * jj4:128 * (jj4 + 1)]:
                            e.activation(o, i, AF.Copy),
                            reads=["ps_tf", "k_xt"], writes=["k_xt"])
                dma("SP", out_d[BLK * ph + 128 * tk:BLK * ph + 128 * (tk + 1), :],
                    s_xt[:], reads=["k_xt"], writes=["d_out"])

        P.emit("SP", "fence", None, reads=["d_out"])

        sec = {"SP": block.sync, "PE": block.tensor, "ACT": block.scalar,
               "DVE": block.vector, "POOL": block.gpsimd}

        def make_body(name):
            def body(e):
                for waits, fn, inc in P.entries[name]:
                    for sem, val in waits:
                        e.wait_ge(sems[sem], val)
                    if fn is not None:
                        inst = fn(e)
                        if inc is not None:
                            s, v = inc
                            if v is None:
                                inst.then_inc(sems[s])
                            else:
                                inst.then_inc(sems[s], v)
            return body

        for name in ENGINES:
            sec[name](make_body(name))

    return nc


_NC_CACHE = [None]


def kernel(**inputs):
    f32 = np.float32
    x = np.asarray(inputs["L0_attn_x"], f32)[0]
    Ltot = x.shape[0]

    def t(name):
        return np.ascontiguousarray(np.asarray(inputs[name], f32))

    ln1_g, ln1_b = t("L0_attn_ln1_g"), t("L0_attn_ln1_b")
    w_in = t("L0_attn_w_in")
    conv_w, conv_b = t("L0_attn_conv_w"), t("L0_attn_conv_b")
    w_x, w_dt, b_dt = t("L0_attn_w_x"), t("L0_attn_w_dt"), t("L0_attn_b_dt")
    a_log, d_skip = t("L0_attn_a_log"), t("L0_attn_d_skip")
    w_out = t("L0_attn_w_out")
    ln2_g, ln2_b = t("L0_attn_ln2_g"), t("L0_attn_ln2_b")
    w_ff1, b_ff1 = t("L0_attn_w_ff1"), t("L0_attn_b_ff1")
    w_ff2, b_ff2 = t("L0_attn_w_ff2"), t("L0_attn_b_ff2")
    G1R, G2R = t("L0_ffn_G1_R"), t("L0_ffn_G2_R")
    G1I, G2I = t("L0_ffn_G1_I"), t("L0_ffn_G2_I")
    br, bi = t("L0_ffn_bias_real"), t("L0_ffn_bias_imag")

    shared = dict(
        ln1g_b=np.ascontiguousarray(np.tile(ln1_g[None, :], (128, 1))),
        ln1b_b=np.ascontiguousarray(np.tile(ln1_b[None, :], (128, 1))),
        w_in_f=w_in,
        conv_w_fm=np.ascontiguousarray(
            conv_w.reshape(16, 128, 4).transpose(1, 0, 2).reshape(128, 64)),
        conv_b_fm=np.ascontiguousarray(conv_b.reshape(16, 128).T),
        w_x_fm=np.ascontiguousarray(
            w_x.reshape(16, 128, 96).transpose(1, 0, 2).reshape(128, 16 * 96)),
        w_dt_f=w_dt,
        b_dt_fm=np.ascontiguousarray(b_dt.reshape(16, 128).T),
        d_skip_fm=np.ascontiguousarray(d_skip.reshape(16, 128).T),
        w_out_f=w_out,
        ln2g_fm=np.ascontiguousarray(ln2_g.reshape(8, 128).T),
        ln2b_fm=np.ascontiguousarray(ln2_b.reshape(8, 128).T),
        w_ff1_f=w_ff1,
        b_ff1_fm=np.ascontiguousarray(b_ff1.reshape(64, 128).T),
        w_ff2_f=w_ff2,
        b_ff2_fm=np.ascontiguousarray(b_ff2.reshape(8, 128).T),
        g1_in=np.ascontiguousarray(np.stack([G1R[0], G1I[0]])),
        g2_perm=np.ascontiguousarray(
            np.stack([G2R[:, :, 0, :].transpose(0, 2, 1),
                      G2I[:, :, 0, :].transpose(0, 2, 1)])),
        br_fm=np.ascontiguousarray(br.reshape(4, 128).T),
        bi_fm=np.ascontiguousarray(bi.reshape(4, 128).T),
        idf=np.eye(128, dtype=f32),
        idb=np.eye(128).astype(ml_dtypes.bfloat16),
        ones_r=np.ones((1, 128), f32),
        half_r=np.full((1, 128), 0.5, f32),
        invd_c=np.full((128, 1), 1.0 / D, f32),
        consts_in=np.tile(np.array([1e-5, 1.0 / D, 0.5, -0.5, -1.0, 1.0,
                                    0.044715, 0.7978845608028654, 2.0, 0.0,
                                    0.25, -2.0, 4.0, 0.125, 8.0, 3.0],
                                   f32)[None, :], (128, 1)),
    )

    in_maps = []
    for core in range(NC):
        xbv = np.zeros((NPH, 3 + BLK, D), f32)
        hm = np.ones((128, NPH), f32)
        for c in range(NPH):
            g = 8 * c + core
            lo = g * BLK
            xbv[c, 3:, :] = x[lo:lo + BLK]
            if g > 0:
                xbv[c, 0:3, :] = x[lo - 3:lo]
            else:
                hm[:, c] = 0.0
        m = dict(shared)
        m["xb"] = xbv
        m["hmask"] = hm
        m["a_log_my"] = np.ascontiguousarray(
            a_log[256 * core:256 * (core + 1)].reshape(2, 128, DS))
        in_maps.append(m)

    if _NC_CACHE[0] is None:
        _NC_CACHE[0] = build_program()
    nc = _NC_CACHE[0]
    res = run_bass_kernel_spmd(nc, in_maps, core_ids=list(range(NC)))

    out = np.zeros((Ltot, D), f32)
    for core in range(NC):
        o = res.results[core]["out"]
        for c in range(NPH):
            g = 8 * c + core
            out[g * BLK:(g + 1) * BLK] = o[c * BLK:(c + 1) * BLK]
    return out.reshape(1, Ltot, D)


if __name__ == "__main__":
    import sys
    d = np.load(sys.argv[1])
    ins = {k: d[k] for k in d.files}
    got = kernel(**ins)
    exp = np.load(sys.argv[2])["exp_0"]
    err = got.astype(np.float64) - exp.astype(np.float64)
    rel = float(np.sqrt((err ** 2).mean() / (exp.astype(np.float64) ** 2).mean()))
    print("rel_l2:", rel, "maxabs:", float(np.abs(err).max()))


# version 05
# speedup vs baseline: 1.1160x; 1.1160x over previous
import numpy as np
import ml_dtypes
import concourse.bass as bass
import concourse.mybir as mybir
from concourse.bass_utils import run_bass_kernel_spmd
from concourse.aot_env import allow_ici_all_to_all
from contextlib import ExitStack

F32 = mybir.dt.float32
BF16 = mybir.dt.bfloat16
AF = mybir.ActivationFunctionType
ALU = mybir.AluOpType
AX = mybir.AxisListType

NC = 8
D = 1024
DI = 2048
DS = 16
DTR = 64
FF = 4096
BLK = 512
NPH = 4
EPS = 1e-5
GC1 = 0.7978845608028654
GC3 = 0.044715

ENGINES = ("SP", "PE", "ACT", "DVE", "POOL")


class Plan:
    def __init__(self):
        self.entries = {e: [] for e in ENGINES}
        self.ccnt = {e: 0 for e in ENGINES}
        self.dcnt = {e: 0 for e in ENGINES}
        self.res = {}
        self.marks = {e: {} for e in ENGINES}
        self.n_coll = 0

    STRICT_SAME_ENGINE = True

    def _add(self, eng, dep, waits, is_dma):
        if dep is None:
            return
        sem, val = dep
        if (not is_dma) and sem == "c:" + eng:
            if eng == "PE" or not Plan.STRICT_SAME_ENGINE:
                return
        if self.marks[eng].get(sem, 0) >= val:
            return
        if waits.get(sem, 0) < val:
            waits[sem] = val

    def global_barrier(self):
        vals = {}
        for e in ENGINES:
            if self.ccnt[e]:
                vals["c:" + e] = self.ccnt[e]
            if self.dcnt[e]:
                vals["d:" + e] = 16 * self.dcnt[e]
        for i in range(self.n_coll):
            vals["x:%d" % i] = 1
        for e in ENGINES:
            waits = [(s, v) for s, v in sorted(vals.items())
                     if s != "c:" + e and self.marks[e].get(s, 0) < v]
            for s, v in waits:
                self.marks[e][s] = v
            self.entries[e].append((waits, None, None))

    BARRIER_EVERY = 0
    _ec = 0

    def emit(self, eng, kind, fn, reads=(), writes=()):
        Plan._ec += 1
        if Plan.BARRIER_EVERY and Plan._ec % Plan.BARRIER_EVERY == 0:
            self.global_barrier()
        is_dma = kind in ("dma", "coll")
        waits = {}
        for k in reads:
            r = self.res.get(k)
            if r:
                self._add(eng, r["w"], waits, is_dma)
        for k in writes:
            r = self.res.get(k)
            if r:
                self._add(eng, r["w"], waits, is_dma)
                for d in r["r"]:
                    self._add(eng, d, waits, is_dma)
        for sem, val in waits.items():
            self.marks[eng][sem] = val
        if kind == "compute":
            self.ccnt[eng] += 1
            comp = ("c:" + eng, self.ccnt[eng])
            inc = ("c:" + eng, 1)
        elif kind == "dma":
            self.dcnt[eng] += 1
            comp = ("d:" + eng, 16 * self.dcnt[eng])
            inc = ("d:" + eng, 16)
        elif kind == "coll":
            comp = ("x:%d" % self.n_coll, 1)
            inc = ("x:%d" % self.n_coll, None)
            self.n_coll += 1
        else:
            comp, inc = None, None
        self.entries[eng].append((sorted(waits.items()), fn, inc))
        for k in writes:
            self.res[k] = {"w": comp, "r": []}
        if comp is not None:
            for k in reads:
                r = self.res.setdefault(k, {"w": None, "r": []})
                r["r"].append(comp)


def build_program():
    nc = bass.Bass("TRN2", target_bir_lowering=False, debug=False)
    P = Plan()

    def inp(name, shape, dtype=F32):
        return nc.dram_tensor(name, list(shape), dtype, kind="ExternalInput").ap()

    xb = inp("xb", [NPH, 3 + BLK, D])
    hmask = inp("hmask", [128, NPH])
    a_log_my = inp("a_log_my", [2, 128, DS])
    ln1g_b = inp("ln1g_b", [128, D])
    ln1b_b = inp("ln1b_b", [128, D])
    w_in_f = inp("w_in_f", [D, 2 * DI])
    conv_w_fm = inp("conv_w_fm", [128, 64])
    conv_b_fm = inp("conv_b_fm", [128, 16])
    w_x_fm = inp("w_x_fm", [128, 16 * 96])
    w_dt_f = inp("w_dt_f", [DTR, DI])
    b_dt_fm = inp("b_dt_fm", [128, 16])
    d_skip_fm = inp("d_skip_fm", [128, 16])
    w_out_f = inp("w_out_f", [DI, D])
    ln2g_fm = inp("ln2g_fm", [128, 8])
    ln2b_fm = inp("ln2b_fm", [128, 8])
    w_ff1_f = inp("w_ff1_f", [D, 2 * FF])
    b_ff1_fm = inp("b_ff1_fm", [128, 64])
    w_ff2_f = inp("w_ff2_f", [FF, D])
    b_ff2_fm = inp("b_ff2_fm", [128, 8])
    g1_in = inp("g1_in", [2, 32, 4, 32])
    g2_perm = inp("g2_perm", [2, 4, 16, 16])
    br_fm = inp("br_fm", [128, 4])
    bi_fm = inp("bi_fm", [128, 4])
    idf = inp("idf", [128, 128])
    idb = inp("idb", [128, 128], BF16)
    ones_r = inp("ones_r", [1, 128])
    half_r = inp("half_r", [1, 128])
    invd_c = inp("invd_c", [128, 1])
    consts_in = inp("consts_in", [128, 16])

    out_d = nc.dram_tensor("out", [NPH * BLK, D], F32, kind="ExternalOutput").ap()

    def idram(name, shape, dtype=F32):
        return nc.dram_tensor(name, list(shape), dtype, kind="Internal")

    w_in_bf = idram("w_in_bf", [D, 2 * DI], BF16)
    w_out_bf = idram("w_out_bf", [DI, D], BF16)
    w_ff1_bf = idram("w_ff1_bf", [D, 2 * FF], BF16)
    w_ff2_bf = idram("w_ff2_bf", [FF, D], BF16)
    a2a_dt_i = [idram(f"a2a_dt_i{c}", [DI, BLK]) for c in range(NPH)]
    a2a_dt_o = [idram(f"a2a_dt_o{c}", [DI, BLK]) for c in range(NPH)]
    a2a_dx_i = [idram(f"a2a_dx_i{c}", [DI, BLK]) for c in range(NPH)]
    a2a_dx_o = [idram(f"a2a_dx_o{c}", [DI, BLK]) for c in range(NPH)]
    a2a_y_i = [idram(f"a2a_y_i{c}", [DI, BLK]) for c in range(NPH)]
    a2a_y_o = [idram(f"a2a_y_o{c}", [DI, BLK]) for c in range(NPH)]
    bc_i = [idram(f"bc_i{c}", [32, BLK]) for c in range(NPH)]
    bc_o = [idram(f"bc_o{c}", [NC * 32, BLK]) for c in range(NPH)]
    x_sp = [idram(f"x_sp{c}", [D, BLK]) for c in range(NPH)]
    dxc_sp = [idram(f"dxc_sp{c}", [DI, BLK]) for c in range(NPH)]
    szu_sp = [idram(f"szu_sp{c}", [DI, BLK]) for c in range(NPH)]
    m_bounce = idram("m_bounce", [512, 512])
    fl_i = [idram(f"fl_i{c}", [1, 16]) for c in range(2 * NPH)]
    fl_o = [idram(f"fl_o{c}", [NC, 16]) for c in range(2 * NPH)]

    rg = [list(range(NC))]

    with ExitStack() as st:
        ec = st.enter_context

        def sbuf(name, shape, dtype=F32):
            return ec(nc.sbuf_tensor(name, list(shape), dtype))

        s_ln1g = sbuf("s_ln1g", [128, D])
        s_ln1b = sbuf("s_ln1b", [128, D])
        s_idf = sbuf("s_idf", [128, 128])
        s_idb = sbuf("s_idb", [128, 128], BF16)
        s_ones = sbuf("s_ones", [1, 128])
        s_half = sbuf("s_half", [1, 128])
        s_invd = sbuf("s_invd", [128, 1])
        s_const = sbuf("s_const", [128, 16])
        s_convw = sbuf("s_convw", [128, 64])
        s_convb = sbuf("s_convb", [128, 16])
        s_thb = sbuf("s_thb", [128, 16])
        s_bdt = sbuf("s_bdt", [128, 16])
        s_dsh = sbuf("s_dsh", [128, 16])
        s_wx = sbuf("s_wx", [128, 16 * 96])
        s_wdt = sbuf("s_wdt", [DTR, DI])
        s_alog = sbuf("s_alog", [128, 32])
        s_A = sbuf("s_A", [128, 32])
        s_carry = sbuf("s_carry", [128, 32])
        s_hm = sbuf("s_hm", [128, NPH])
        s_ln2g = sbuf("s_ln2g", [128, 8])
        s_ln2b = sbuf("s_ln2b", [128, 8])
        s_bff1 = sbuf("s_bff1", [128, 64])
        s_bff2 = sbuf("s_bff2", [128, 8])
        s_br = sbuf("s_br", [128, 4])
        s_bi = sbuf("s_bi", [128, 4])
        s_wrt = sbuf("s_wrt", [128, 4 * 512], BF16)
        s_wit = sbuf("s_wit", [128, 4 * 512], BF16)
        s_witn = sbuf("s_witn", [128, 4 * 512], BF16)

        A_N, A_W = 5, 515
        poolA = sbuf("poolA", [128, A_N * A_W])
        poolB = sbuf("poolB", [128, 16 * BLK])
        poolC = sbuf("poolC", [128, 16 * A_W], BF16)
        poolD = sbuf("poolD", [128, 32 * BLK], BF16)
        s_xfm = sbuf("s_xfm", [128, 8 * BLK])
        s_xt = sbuf("s_xt", [128, D])
        s_lnx = sbuf("s_lnx", [128, D])
        s_hx = sbuf("s_hx", [128, D], BF16)
        s_cols = sbuf("s_cols", [128, 16])
        s_dlo = sbuf("s_dlo", [DTR, BLK])
        s_bc32 = sbuf("s_bc32", [32, BLK])
        s_bcf = sbuf("s_bcf", [1, 8 * BLK])
        s_Bb = sbuf("s_Bb", [128, BLK])
        s_Cb = sbuf("s_Cb", [128, BLK])
        s_w1 = sbuf("s_w1", [128, 2 * BLK])
        s_row = sbuf("s_row", [1, 2 * BLK])
        pnl = sbuf("pnl", [128, 2 * 1024], BF16)
        stg_f = sbuf("stg_f", [128, 512])
        stg_b = sbuf("stg_b", [128, 512], BF16)
        s_mub = s_Bb
        s_rsb = s_Cb
        s_dseg = poolB[:, 0:2 * BLK]
        s_xseg = poolB[:, 2 * BLK:4 * BLK]
        s_ys = poolB[:, 4 * BLK:6 * BLK]
        s_hbuf = poolB[:, 6 * BLK:9 * BLK]

        p_big0 = ec(nc.psum_tensor("p_big0", [128, 512], F32))
        p_big1 = ec(nc.psum_tensor("p_big1", [128, 512], F32))
        p_aux0 = ec(nc.psum_tensor("p_aux0", [128, 512], F32))
        p_aux1 = ec(nc.psum_tensor("p_aux1", [128, 512], F32))
        p_trf = ec(nc.psum_tensor("p_trf", [128, 512], F32))
        p_trb = ec(nc.psum_tensor("p_trb", [128, 1024], BF16))
        p_stat = ec(nc.psum_tensor("p_stat", [1, 512], F32))

        sems = {}
        for e in ENGINES:
            sems["c:" + e] = ec(nc.semaphore("sc_" + e))
            sems["d:" + e] = ec(nc.semaphore("sd_" + e))
        for i in range(8 * NPH):
            sems["x:%d" % i] = ec(nc.semaphore("sx_%d" % i))

        block = ec(nc.Block())

        def dma(q, out, in_, reads=(), writes=()):
            P.emit(q, "dma", lambda e, o=out, i=in_: e.dma_start(out=o, in_=i),
                   reads=list(reads), writes=list(writes))

        def cop(eng, fn, reads=(), writes=()):
            P.emit(eng, "compute", fn, reads=list(reads), writes=list(writes))

        def coll(kind, ins, outs, reads=(), writes=()):
            def fn(e, k=kind, ii=ins, oo=outs):
                if k == "AllToAll":
                    with allow_ici_all_to_all():
                        return e.collective_compute(k, ALU.bypass, replica_groups=rg,
                                                    ins=ii, outs=oo)
                return e.collective_compute(k, ALU.bypass, replica_groups=rg,
                                            ins=ii, outs=oo)
            P.emit("POOL", "coll", fn, reads=list(reads), writes=list(writes))

        def pa(i, w=BLK):
            return poolA[:, i * A_W:i * A_W + w], f"pA{i}"

        def pb(i, w=BLK):
            return poolB[:, i * BLK:i * BLK + w], f"pB{i}"

        def pc(i, w=BLK):
            return poolC[:, i * A_W:i * A_W + w], f"pC{i}"

        def pd(i, w=BLK):
            return poolD[:, i * BLK:i * BLK + w], f"pD{i}"

        CONST_VALS = [EPS, 1.0 / D, 0.5, -0.5, -1.0, 1.0, GC3, GC1, 2.0, 0.0,
                      0.25, -2.0, 4.0, 0.125, 8.0, 3.0]
        for ci, cv in enumerate(CONST_VALS):
            nc.const_aps.register(F32, cv, s_const[:, ci:ci + 1])
        dma("SP", s_const[:], consts_in, writes=["k_const"])
        for _e in ENGINES:
            P.emit(_e, "fence", None, reads=["k_const"])
        for wf in (AF.Exp, AF.Ln, AF.Tanh, AF.Abs, AF.Relu, AF.Square, AF.Copy):
            cop("ACT", lambda e, f=wf: e.activation(s_cols[0:1, 5:6],
                                                    s_const[0:1, 5:6], f),
                reads=["k_const", "k_warm"], writes=["k_warm"])
        for _e in ENGINES:
            P.emit(_e, "fence", None, reads=["k_warm"])
        for (dst, src, key) in [
            (s_ln1g[:], ln1g_b, "k_ln1g"), (s_ln1b[:], ln1b_b, "k_ln1b"),
            (s_idf[:], idf, "k_idf"), (s_idb[:], idb, "k_idb"),
            (s_ones[:], ones_r, "k_ones"), (s_half[:], half_r, "k_half"),
            (s_invd[:], invd_c, "k_invd"),
            (s_convw[:], conv_w_fm, "k_convw"), (s_convb[:], conv_b_fm, "k_convb"),
            (s_bdt[:], b_dt_fm, "k_bdt"), (s_dsh[:], d_skip_fm, "k_dsh"),
            (s_wx[:], w_x_fm, "k_wx"), (s_wdt[:], w_dt_f, "k_wdt"),
            (s_hm[:], hmask, "k_hm"),
            (s_ln2g[:], ln2g_fm, "k_ln2g"), (s_ln2b[:], ln2b_fm, "k_ln2b"),
            (s_bff1[:], b_ff1_fm, "k_bff1"), (s_bff2[:], b_ff2_fm, "k_bff2"),
            (s_br[:], br_fm, "k_br"), (s_bi[:], bi_fm, "k_bi"),
        ]:
            dma("SP", dst, src, writes=[key])
        dma("SP", s_alog[:, 0:16], a_log_my[0], writes=["k_alog"])
        dma("SP", s_alog[:, 16:32], a_log_my[1], writes=["k_alog2"])

        cop("ACT", lambda e: e.activation(s_A[:], s_alog[:], AF.Exp),
            reads=["k_alog", "k_alog2"], writes=["k_A"])
        cop("DVE", lambda e: e.tensor_scalar_mul(s_A[:], s_A[:], -1.0),
            reads=["k_A"], writes=["k_A"])
        cop("DVE", lambda e: e.tensor_scalar_mul(s_thb[:], s_convb[:], 0.5),
            reads=["k_convb"], writes=["k_thb"])
        cop("DVE", lambda e: e.tensor_scalar_mul(s_dsh[:], s_dsh[:], 0.5),
            reads=["k_dsh"], writes=["k_dsh"])
        cop("DVE", lambda e: e.tensor_scalar_mul(s_wx[:], s_wx[:], 0.5),
            reads=["k_wx"], writes=["k_wx"])
        cop("DVE", lambda e: e.memset(s_carry[:], 0.0), writes=["k_carry"])

        def cast_weight(src_ap, dst_dram, rows, cols, key):
            for r in range(rows // 128):
                for cc in range(cols // 512):
                    fs, bs, kf, kb = stg_f[:], stg_b[:], "stgf", "stgb"
                    dma("SP", fs, src_ap[128 * r:128 * (r + 1), 512 * cc:512 * (cc + 1)],
                        reads=[kf], writes=[kf])
                    cop("POOL", lambda e, a=bs, b=fs: e.tensor_copy(a, b),
                        reads=[kf, kb], writes=[kb])
                    dma("ACT", dst_dram.ap()[128 * r:128 * (r + 1), 512 * cc:512 * (cc + 1)],
                        bs, reads=[kb], writes=[key])

        cast_weight(w_in_f, w_in_bf, D, 2 * DI, "k_winbf")
        cast_weight(w_out_f, w_out_bf, DI, D, "k_woutbf")
        cast_weight(w_ff1_f, w_ff1_bf, D, 2 * FF, "k_wff1bf")
        cast_weight(w_ff2_f, w_ff2_bf, FF, D, "k_wff2bf")

        pM = [p_big0, p_big1, p_aux0, p_aux1]
        pM_k = ["ps_b0", "ps_b1", "ps_a0", "ps_a1"]
        s_m0 = poolB[:, 0:2048]
        s_bda = poolB[:, 2048:4096]
        M0K = ["pB0", "pB1", "pB2", "pB3"]
        BDK = ["pB4", "pB5", "pB6", "pB7"]
        for ri in range(2):
            for c4 in range(4):
                cop("DVE", lambda e: e.memset(s_m0[:], 0.0),
                    reads=M0K, writes=M0K)
                for i32 in range(32):
                    tile, prt = divmod(16 * i32, 128)
                    dma("SP", s_m0[prt:prt + 16,
                                   tile * 512 + 16 * i32:tile * 512 + 16 * i32 + 16],
                        g2_perm[ri, c4], reads=M0K, writes=M0K)
                dma("SP", m_bounce.ap().rearrange("(t p) c -> p t c", p=128),
                    s_m0.rearrange("p (t c) -> p t c", c=512),
                    reads=M0K, writes=["k_mb"])
                prm = m_bounce.ap().rearrange("(d g) c -> g d c", d=32, g=16)
                for t4 in range(4):
                    dma("SP", s_m0[:, t4 * 512:(t4 + 1) * 512],
                        prm[4 * t4:4 * (t4 + 1), :, :],
                        reads=["k_mb"] + M0K, writes=M0K)
                cop("DVE", lambda e: e.memset(s_bda[:], 0.0),
                    reads=BDK, writes=BDK)
                for i16 in range(16):
                    tile, prt = divmod(32 * i16, 128)
                    dma("SP", s_bda[prt:prt + 32,
                                    tile * 512 + 32 * i16:tile * 512 + 32 * i16 + 32],
                        g1_in[ri, :, c4, :], reads=BDK, writes=BDK)
                for it in range(4):
                    for kt in range(4):
                        lh = s_bda[:, kt * 512 + 128 * it:kt * 512 + 128 * it + 128]
                        rh = s_m0[:, kt * 512:(kt + 1) * 512]
                        cop("PE", lambda e, o=pM[it][:, 0:512], a=lh, b=rh,
                            f=(c4 == 0 and kt == 0), l=(c4 == 3 and kt == 3):
                            e.matmul(o, a, b, start=f, stop=l),
                            reads=BDK + M0K, writes=[pM_k[it]])
            for it in range(4):
                cop("ACT", lambda e, o=s_m0[:, it * 512:(it + 1) * 512],
                    i=pM[it][:, 0:512]: e.activation(o, i, AF.Copy),
                    reads=[pM_k[it]] + M0K, writes=M0K)
            dma("SP", m_bounce.ap().rearrange("(t p) c -> p t c", p=128),
                s_m0.rearrange("p (t c) -> p t c", c=512),
                reads=M0K, writes=["k_mb"])
            prm2 = m_bounce.ap().rearrange("(g d) c -> d g c", g=16, d=32)
            for t4 in range(4):
                dma("SP", s_bda[:, t4 * 512:(t4 + 1) * 512],
                    prm2[8 * t4:8 * (t4 + 1), :, :],
                    reads=["k_mb"] + BDK, writes=BDK)
            dstb, kk = (s_wrt, "k_wrt") if ri == 0 else (s_wit, "k_wit")
            for t4 in range(4):
                cop("POOL", lambda e, o=dstb[:, t4 * 512:(t4 + 1) * 512],
                    i=s_bda[:, t4 * 512:(t4 + 1) * 512]: e.tensor_copy(o, i),
                    reads=BDK + [kk], writes=[kk])
            if ri == 1:
                cop("DVE", lambda e: e.tensor_scalar_mul(s_bda, s_bda, -1.0),
                    reads=BDK, writes=BDK)
                for t4 in range(4):
                    cop("POOL", lambda e, o=s_witn[:, t4 * 512:(t4 + 1) * 512],
                        i=s_bda[:, t4 * 512:(t4 + 1) * 512]: e.tensor_copy(o, i),
                        reads=BDK + ["k_witn"], writes=["k_witn"])

        P.global_barrier()
        P.global_barrier()

        for ph in range(NPH):
            for tt in range(5):
                rows = 128 if tt < 4 else 3
                xsl = s_xt[0:rows, :]
                xm = s_lnx[0:rows, :]
                hsl = s_hx[0:rows, :]
                kx, kxm, kh = "k_xt", "k_lnx", "k_hx"
                src = xb[ph, 3 + 128 * tt:3 + 128 * (tt + 1), :] if tt < 4 else xb[ph, 0:3, :]
                dma("SP", xsl, src, reads=[kx], writes=[kx])
                if tt < 4:
                    for half in range(2):
                        for dk4 in range(4):
                            dk = half * 4 + dk4
                            cop("PE", lambda e, o=p_trf[:, 128 * dk4:128 * (dk4 + 1)],
                                i=xsl[:, 128 * dk:128 * (dk + 1)], idn=s_idf[:]:
                                e.transpose(o, i, idn),
                                reads=[kx, "k_idf", "ps_tf"], writes=["ps_tf"])
                        for dk4 in range(4):
                            dk = half * 4 + dk4
                            cop("ACT", lambda e,
                                o=s_xfm[:, dk * BLK + 128 * tt:dk * BLK + 128 * (tt + 1)],
                                i=p_trf[:, 128 * dk4:128 * (dk4 + 1)]:
                                e.activation(o, i, AF.Copy),
                                reads=["ps_tf", f"k_xfm{dk}"], writes=[f"k_xfm{dk}"])
                sm = s_cols[0:rows, 0:1]
                mu = s_cols[0:rows, 1:2]
                vv = s_cols[0:rows, 2:3]
                r0 = s_cols[0:rows, 3:4]
                rs = s_cols[0:rows, 4:5]
                cop("ACT", lambda e, o=hsl, i=xsl, a=sm:
                    e.activation(o, i, AF.Copy, accum_out=a),
                    reads=[kx, kh, "k_c0"], writes=[kh, "k_c0"])
                cop("DVE", lambda e, o=mu, i=sm: e.tensor_scalar_mul(o, i, 1.0 / D),
                    reads=["k_c0", "k_c1"], writes=["k_c1"])
                cop("DVE", lambda e, o=xm, i=xsl, m=mu:
                    e.tensor_scalar(o, i, m, None, ALU.subtract),
                    reads=[kx, "k_c1", kxm], writes=[kxm])
                cop("ACT", lambda e, o=hsl, i=xm, a=vv:
                    e.activation(o, i, AF.Square, accum_out=a),
                    reads=[kxm, kh, "k_c2"], writes=[kh, "k_c2"])
                cop("ACT", lambda e, o=r0, i=vv:
                    e.activation(o, i, AF.Ln, bias=EPS, scale=1.0 / D),
                    reads=["k_c2", "k_c3"], writes=["k_c3"])
                cop("ACT", lambda e, o=rs, i=r0:
                    e.activation(o, i, AF.Exp, scale=-0.5),
                    reads=["k_c3", "k_c4"], writes=["k_c4"])
                cop("DVE", lambda e, o=xsl, i=xm, m=rs:
                    e.tensor_scalar(o, i, m, None, ALU.mult),
                    reads=[kxm, "k_c4", kx], writes=[kx])
                cop("DVE", lambda e, o=xm, a=xsl, g=s_ln1g[0:rows, :]:
                    e.tensor_mul(o, a, g),
                    reads=[kx, "k_ln1g", kxm], writes=[kxm])
                cop("DVE", lambda e, o=hsl, a=xm, b=s_ln1b[0:rows, :]:
                    e.tensor_add(o, a, b),
                    reads=[kxm, "k_ln1b", kh], writes=[kh])
                off = 3 + 128 * tt if tt < 4 else 0
                for dk in range(8):
                    cop("PE", lambda e, o=p_trb[:, 128 * dk:128 * dk + rows],
                        i=hsl[:, 128 * dk:128 * (dk + 1)], idn=s_idb[0:rows, 0:rows]:
                        e.transpose(o, i, idn),
                        reads=[kh, "k_idb", "ps_tb"], writes=["ps_tb"])
                for dk in range(8):
                    cop("ACT", lambda e,
                        o=poolC[:, dk * A_W + off:dk * A_W + off + rows],
                        i=p_trb[:, 128 * dk:128 * dk + rows]:
                        e.activation(o, i, AF.Copy),
                        reads=["ps_tb", f"pC{dk}"], writes=[f"pC{dk}"])
            xfm_keys = [f"k_xfm{dk}" for dk in range(8)]
            dma("ACT", x_sp[ph].ap().rearrange("(j p) t -> p j t", p=128),
                s_xfm[:].rearrange("p (j t) -> p j t", t=BLK),
                reads=xfm_keys, writes=["d_xsp"])

            hfm_keys = [f"pC{dk}" for dk in range(8)]

            for ot in range(32):
                pg, kpg = (p_big0, "ps_b0") if ot % 2 == 0 else (p_big1, "ps_b1")
                kpn = f"k_pnl{ot % 2}"
                pn = pnl[:, (ot % 2) * 1024:(ot % 2) * 1024 + 1024]
                dma("SP", pn.rearrange("p (kt c) -> p kt c", kt=8),
                    w_in_bf.ap().rearrange("(kt p) c -> p kt c", p=128)[:, :, 128 * ot:128 * (ot + 1)],
                    reads=["k_winbf", kpn], writes=[kpn])
                for kt in range(8):
                    cop("PE", lambda e, o=pg[:, 0:512], a=pn[:, kt * 128:(kt + 1) * 128],
                        b=poolC[:, kt * A_W + 3:kt * A_W + 515],
                        f=(kt == 0), l=(kt == 7):
                        e.matmul(o, a, b, start=f, stop=l),
                        reads=hfm_keys + [kpn], writes=[kpg])
                if ot < 16:
                    for kt in range(8):
                        cop("PE", lambda e, o=p_trf[:, 0:3], a=pn[:, kt * 128:(kt + 1) * 128],
                            b=poolC[:, kt * A_W:kt * A_W + 3],
                            f=(kt == 0), l=(kt == 7):
                            e.matmul(o, a, b, start=f, stop=l),
                            reads=hfm_keys + [kpn], writes=["ps_tf"])
                    j = ot
                    xzc, kxzc = pa(j % 2, 515)
                    cop("ACT", lambda e, o=xzc[:, 3:515], i=pg[:, 0:512]:
                        e.activation(o, i, AF.Copy),
                        reads=[kpg, kxzc], writes=[kxzc])
                    cop("ACT", lambda e, o=xzc[:, 0:3], i=p_trf[:, 0:3]:
                        e.activation(o, i, AF.Copy),
                        reads=["ps_tf", kxzc], writes=[kxzc])
                    cop("DVE", lambda e, o=xzc[:, 0:3], m=s_hm[:, ph:ph + 1]:
                        e.tensor_scalar(o, o, m, None, ALU.mult),
                        reads=["k_hm", kxzc], writes=[kxzc])
                    acc0 = s_w1[:, 0:BLK]
                    acc1 = s_w1[:, BLK:2 * BLK]
                    cop("DVE", lambda e, o=acc0, i=xzc[:, 0:512], w=s_convw[:, 4 * j:4 * j + 1]:
                        e.tensor_scalar(o, i, w, None, ALU.mult),
                        reads=[kxzc, "k_convw", "k_w10"], writes=["k_w10"])
                    cop("DVE", lambda e, o=acc1, i=xzc[:, 1:513],
                        w=s_convw[:, 4 * j + 1:4 * j + 2], a=acc0:
                        e.scalar_tensor_tensor(o, i, w, a, ALU.mult, ALU.add),
                        reads=[kxzc, "k_convw", "k_w10", "k_w11"], writes=["k_w11"])
                    cop("DVE", lambda e, o=acc0, i=xzc[:, 2:514],
                        w=s_convw[:, 4 * j + 2:4 * j + 3], a=acc1:
                        e.scalar_tensor_tensor(o, i, w, a, ALU.mult, ALU.add),
                        reads=[kxzc, "k_convw", "k_w11", "k_w10"], writes=["k_w10"])
                    cop("DVE", lambda e, o=acc1, i=xzc[:, 3:515],
                        w=s_convw[:, 4 * j + 3:4 * j + 4], a=acc0:
                        e.scalar_tensor_tensor(o, i, w, a, ALU.mult, ALU.add),
                        reads=[kxzc, "k_convw", "k_w10", "k_w11"], writes=["k_w11"])
                    th, kth = pa(2)
                    cop("ACT", lambda e, o=th, i=acc1, b=s_thb[:, j:j + 1]:
                        e.activation(o, i, AF.Tanh, scale=0.5, bias=b),
                        reads=["k_w11", "k_thb", kth], writes=[kth])
                    vt, kvt = pa(3)
                    cop("DVE", lambda e, o=vt, i=acc1, b=s_convb[:, j:j + 1], t=th:
                        e.scalar_tensor_tensor(o, i, b, t, ALU.add, ALU.mult),
                        reads=["k_w11", "k_convb", kth, kvt], writes=[kvt])
                    xcu, kxcu = pb(j)
                    cop("DVE", lambda e, o=xcu, i=acc1, b=s_convb[:, j:j + 1], t=vt:
                        e.scalar_tensor_tensor(o, i, b, t, ALU.add, ALU.add),
                        reads=["k_w11", "k_convb", kvt, kxcu], writes=[kxcu])
                    dx, kdx = pa(4)
                    cop("POOL", lambda e, o=dx, i=xcu, m=s_dsh[:, j:j + 1]:
                        e.tensor_scalar(o, i, m, None, ALU.mult),
                        reads=[kxcu, "k_dsh", kdx], writes=[kdx])
                    dma("ACT", dxc_sp[ph].ap()[128 * j:128 * (j + 1), :], dx,
                        reads=[kdx], writes=["d_dxc"])
                else:
                    jz = ot - 16
                    th, kth = pa(2)
                    cop("ACT", lambda e, o=th, i=pg[:, 0:512]:
                        e.activation(o, i, AF.Tanh, scale=0.5),
                        reads=[kpg, kth], writes=[kth])
                    t1, kt1 = pa(3)
                    cop("DVE", lambda e, o=t1, a=pg[:, 0:512], b=th: e.tensor_mul(o, a, b),
                        reads=[kpg, kth, kt1], writes=[kt1])
                    sz, ksz = pa(4)
                    cop("DVE", lambda e, o=sz, a=pg[:, 0:512], b=t1: e.tensor_add(o, a, b),
                        reads=[kpg, kt1, ksz], writes=[ksz])
                    dma("ACT", szu_sp[ph].ap()[128 * jz:128 * (jz + 1), :], sz,
                        reads=[ksz], writes=["d_szu"])

            xcu_keys = [f"pB{j}" for j in range(16)]

            for j in range(16):
                cop("PE", lambda e, o=p_aux0[0:64, 0:512],
                    a=s_wx[:, 96 * j:96 * j + 64], b=pb(j)[0],
                    f=(j == 0), l=(j == 15):
                    e.matmul(o, a, b, start=f, stop=l),
                    reads=xcu_keys + ["k_wx"], writes=["ps_a0"])
            for j in range(16):
                cop("PE", lambda e, o=p_aux1[0:32, 0:512],
                    a=s_wx[:, 96 * j + 64:96 * (j + 1)], b=pb(j)[0],
                    f=(j == 0), l=(j == 15):
                    e.matmul(o, a, b, start=f, stop=l),
                    reads=xcu_keys + ["k_wx"], writes=["ps_a1"])
            cop("ACT", lambda e: e.activation(s_dlo[:], p_aux0[0:64, 0:512], AF.Copy),
                reads=["ps_a0", "k_dlo"], writes=["k_dlo"])
            cop("ACT", lambda e: e.activation(s_bc32[:], p_aux1[0:32, 0:512], AF.Copy),
                reads=["ps_a1", "k_bc32"], writes=["k_bc32"])
            dma("ACT", bc_i[ph].ap(), s_bc32[:], reads=["k_bc32"], writes=["d_bci"])

            for j in range(16):
                pg, kpg = (p_big0, "ps_b0") if j % 2 == 0 else (p_big1, "ps_b1")
                cop("PE", lambda e, o=pg[:, 0:512], a=s_wdt[:, 128 * j:128 * (j + 1)],
                    b=s_dlo[:]:
                    e.matmul(o, a, b, start=True, stop=True),
                    reads=["k_dlo", "k_wdt"], writes=[kpg])
                ab = s_w1[:, 0:BLK]
                ee = s_w1[:, BLK:2 * BLK]
                cop("ACT", lambda e, o=ab, i=pg[:, 0:512], b=s_bdt[:, j:j + 1]:
                    e.activation(o, i, AF.Abs, bias=b),
                    reads=[kpg, "k_bdt", "k_w10"], writes=["k_w10"])
                cop("ACT", lambda e, o=ee, i=ab: e.activation(o, i, AF.Exp, scale=-1.0),
                    reads=["k_w10", "k_w11"], writes=["k_w11"])
                cop("ACT", lambda e, o=ab, i=ee: e.activation(o, i, AF.Ln, bias=1.0),
                    reads=["k_w11", "k_w10"], writes=["k_w10"])
                rl, krl = pa(2)
                cop("ACT", lambda e, o=rl, i=pg[:, 0:512], b=s_bdt[:, j:j + 1]:
                    e.activation(o, i, AF.Relu, bias=b),
                    reads=[kpg, "k_bdt", krl], writes=[krl])
                dtt, kdtt = pa(3)
                cop("DVE", lambda e, o=dtt, a=ab, b=rl: e.tensor_add(o, a, b),
                    reads=["k_w10", krl, kdtt], writes=[kdtt])
                dma("SP", a2a_dt_i[ph].ap()[128 * j:128 * (j + 1), :], dtt,
                    reads=[kdtt], writes=["d_dti"])
                dxt, kdxt = pa(4)
                cop("DVE", lambda e, o=dxt, a=dtt, b=pb(j)[0]: e.tensor_mul(o, a, b),
                    reads=[kdtt, f"pB{j}", kdxt], writes=[kdxt])
                dma("SP", a2a_dx_i[ph].ap()[128 * j:128 * (j + 1), :], dxt,
                    reads=[kdxt], writes=["d_dxi"])

            coll("AllToAll", [a2a_dt_i[ph].ap()], [a2a_dt_o[ph].ap()],
                 reads=["d_dti"], writes=[f"c_dt{ph}"])
            coll("AllToAll", [a2a_dx_i[ph].ap()], [a2a_dx_o[ph].ap()],
                 reads=["d_dxi"], writes=[f"c_dx{ph}"])
            coll("AllGather", [bc_i[ph].ap()], [bc_o[ph].ap()],
                 reads=["d_bci"], writes=[f"c_bc{ph}"])
            coll("AllGather", [fl_i[2 * ph].ap()], [fl_o[2 * ph].ap()],
                 reads=[f"c_dt{ph}", f"c_dx{ph}", f"c_bc{ph}"],
                 writes=[f"c_fl{ph}"])

            QSUB = 2048
            import os as _os
            _segbar = bool(_os.environ.get("KSEG_BARRIER"))
            for s8 in range(8):
                if _segbar:
                    P.global_barrier()
                kd = ["pB0", "pB1"]
                kx2 = ["pB2", "pB3"]
                dma("SP", s_dseg[:].rearrange("p (u t) -> p u t", u=2),
                    a2a_dt_o[ph].ap().rearrange("(u p) t -> p u t", p=128)[:, 2 * s8:2 * s8 + 2, :],
                    reads=[f"c_dt{ph}", f"c_fl{ph}"] + kd, writes=kd)
                dma("SP", s_xseg[:].rearrange("p (u t) -> p u t", u=2),
                    a2a_dx_o[ph].ap().rearrange("(u p) t -> p u t", p=128)[:, 2 * s8:2 * s8 + 2, :],
                    reads=[f"c_dx{ph}", f"c_fl{ph}"] + kx2, writes=kx2)
                for ng in range(4):
                    dma("SP", s_bcf[0:1, 0:4 * BLK],
                        bc_o[ph].ap()[32 * s8 + 4 * ng:32 * s8 + 4 * ng + 4, :],
                        reads=[f"c_bc{ph}", f"c_fl{ph}", "k_bcf"], writes=["k_bcf"])
                    dma("SP", s_bcf[0:1, 4 * BLK:8 * BLK],
                        bc_o[ph].ap()[32 * s8 + 16 + 4 * ng:32 * s8 + 16 + 4 * ng + 4, :],
                        reads=[f"c_bc{ph}", f"c_fl{ph}", "k_bcf"], writes=["k_bcf"])
                    for n4 in range(4):
                        n = 4 * ng + n4
                        kB, kC = "k_Bb", "k_Cb"
                        cop("PE", lambda e, o=p_aux0[:, 0:512], a=s_half[:],
                            b=s_bcf[0:1, n4 * BLK:(n4 + 1) * BLK]:
                            e.matmul(o, a, b, start=True, stop=True),
                            reads=["k_bcf", "k_half"], writes=["ps_a0"])
                        cop("ACT", lambda e, o=s_Bb[:], i=p_aux0[:, 0:512]:
                            e.activation(o, i, AF.Copy),
                            reads=["ps_a0", kB], writes=[kB])
                        cop("PE", lambda e, o=p_aux1[:, 0:512], a=s_ones[:],
                            b=s_bcf[0:1, (4 + n4) * BLK:(5 + n4) * BLK]:
                            e.matmul(o, a, b, start=True, stop=True),
                            reads=["k_bcf", "k_ones"], writes=["ps_a1"])
                        cop("ACT", lambda e, o=s_Cb[:], i=p_aux1[:, 0:512]:
                            e.activation(o, i, AF.Copy),
                            reads=["ps_a1", kC], writes=[kC])
                        for sub in range(2):
                            da = s_lnx[:, sub * BLK:(sub + 1) * BLK]
                            kda = "k_lnx"
                            cop("ACT", lambda e, o=da,
                                i=s_dseg[:, sub * BLK:(sub + 1) * BLK],
                                m=s_A[:, sub * 16 + n:sub * 16 + n + 1]:
                                e.activation(o, i, AF.Exp, scale=m),
                                reads=kd + ["k_A", kda], writes=[kda])
                            ub = s_xt[:, sub * BLK:(sub + 1) * BLK]
                            kub = "k_xt"
                            cop("DVE", lambda e, o=ub,
                                a=s_xseg[:, sub * BLK:(sub + 1) * BLK], b=s_Bb[:]:
                                e.tensor_mul(o, a, b),
                                reads=kx2 + [kB, kub], writes=[kub])
                            hslot = (2 * n + sub) % 3
                            hb = s_hbuf[:, hslot * BLK:(hslot + 1) * BLK]
                            khb = f"pB{6 + hslot}"
                            ckey = f"k_cr{sub}_{n}"
                            cop("DVE", lambda e, o=hb, a=da, b=ub,
                                i=s_carry[:, sub * 16 + n:sub * 16 + n + 1]:
                                e.tensor_tensor_scan(o, a, b, i, ALU.mult, ALU.add),
                                reads=[kda, kub, ckey, "k_carry", khb], writes=[khb])
                            cop("DVE", lambda e,
                                o=s_carry[:, sub * 16 + n:sub * 16 + n + 1],
                                i=hb[:, BLK - 1:BLK]: e.tensor_copy(o, i),
                                reads=[khb], writes=[ckey])
                            qkeys = [f"k_xfm{4 * sub + i}" for i in range(4)]
                            qv = s_xfm[:, sub * QSUB:(sub + 1) * QSUB] \
                                .rearrange("p (t k) -> p t k", k=4)[:, :, n4]
                            cop("DVE", lambda e, o=qv, a=hb, b=s_Cb[:]:
                                e.tensor_mul(o, a, b),
                                reads=[khb, kC] + qkeys, writes=qkeys)
                    for sub in range(2):
                        ysub = s_ys[:, sub * BLK:(sub + 1) * BLK]
                        kys = f"pB{4 + sub}"
                        qkeys = [f"k_xfm{4 * sub + i}" for i in range(4)]
                        qv = s_xfm[:, sub * QSUB:(sub + 1) * QSUB] \
                            .rearrange("p (t k) -> p t k", k=4)
                        if ng == 0:
                            cop("DVE", lambda e, o=ysub, i=qv:
                                e.tensor_reduce(o, i, AX.X, ALU.add),
                                reads=qkeys + [kys], writes=[kys])
                        else:
                            cop("DVE", lambda e, o=s_w1[:, 0:BLK], i=qv:
                                e.tensor_reduce(o, i, AX.X, ALU.add),
                                reads=qkeys + ["k_w10"], writes=["k_w10"])
                            cop("DVE", lambda e, o=ysub, a=ysub, b=s_w1[:, 0:BLK]:
                                e.tensor_add(o, a, b),
                                reads=["k_w10", kys], writes=[kys])
                for sub in range(2):
                    dma("SP",
                        a2a_y_i[ph].ap()[256 * s8 + 128 * sub:256 * s8 + 128 * (sub + 1), :],
                        s_ys[:, sub * BLK:(sub + 1) * BLK],
                        reads=[f"pB{4 + sub}"], writes=["d_yi"])
            coll("AllToAll", [a2a_y_i[ph].ap()], [a2a_y_o[ph].ap()],
                 reads=["d_yi"], writes=[f"c_y{ph}"])
            coll("AllGather", [fl_i[2 * ph + 1].ap()], [fl_o[2 * ph + 1].ap()],
                 reads=[f"c_y{ph}"], writes=[f"c_yfl{ph}"])

            dma("SP", s_xfm[:].rearrange("p (j t) -> p j t", j=8),
                x_sp[ph].ap().rearrange("(j p) t -> p j t", p=128),
                reads=["d_xsp"] + xfm_keys, writes=xfm_keys)

            for j in range(16):
                yt, kyt = pa(j % 2)
                dma("SP", yt, a2a_y_o[ph].ap()[128 * j:128 * (j + 1), :],
                    reads=[f"c_y{ph}", f"c_yfl{ph}", kyt], writes=[kyt])
                dx, kdx = pa(2)
                dma("SP", dx, dxc_sp[ph].ap()[128 * j:128 * (j + 1), :],
                    reads=["d_dxc", kdx], writes=[kdx])
                sz, ksz = pa(3)
                dma("SP", sz, szu_sp[ph].ap()[128 * j:128 * (j + 1), :],
                    reads=["d_szu", ksz], writes=[ksz])
                tt1 = s_w1[:, 0:BLK]
                cop("DVE", lambda e, o=tt1, a=yt, b=dx: e.tensor_add(o, a, b),
                    reads=[kyt, kdx, "k_w10"], writes=["k_w10"])
                yd, kyd = pc(j)
                cop("DVE", lambda e, o=yd, i=tt1, z=sz:
                    e.scalar_tensor_tensor(o, i, 0.5, z, ALU.mult, ALU.mult),
                    reads=["k_w10", ksz, kyd], writes=[kyd])
            ydot_keys = [f"pC{j}" for j in range(16)]

            for ot in range(8):
                pg, kpg = (p_big0, "ps_b0") if ot % 2 == 0 else (p_big1, "ps_b1")
                for grp in range(2):
                    kpn = f"k_pnl{grp}"
                    pn = pnl[:, grp * 1024:(grp + 1) * 1024]
                    dma("SP", pn.rearrange("p (kt c) -> p kt c", kt=8),
                        w_out_bf.ap().rearrange("(kt p) c -> p kt c", p=128)
                        [:, 8 * grp:8 * (grp + 1), 128 * ot:128 * (ot + 1)],
                        reads=["k_woutbf", kpn], writes=[kpn])
                    for kt in range(8):
                        cop("PE", lambda e, o=pg[:, 0:512], a=pn[:, kt * 128:(kt + 1) * 128],
                            b=pc(8 * grp + kt)[0],
                            f=(grp == 0 and kt == 0), l=(grp == 1 and kt == 7):
                            e.matmul(o, a, b, start=f, stop=l),
                            reads=ydot_keys + [kpn], writes=[kpg])
                xa, kxa = pb(ot)
                cop("DVE", lambda e, o=xa, i=pg[:, 0:512],
                    x=s_xfm[:, ot * BLK:(ot + 1) * BLK]:
                    e.scalar_tensor_tensor(o, i, 1.0, x, ALU.mult, ALU.add),
                    reads=[kpg, f"k_xfm{ot}", kxa], writes=[kxa])

            def fm_ln(src0, dst0, affine):
                skeys = [f"pB{src0 + j}" for j in range(8)]
                dkeys = [f"pB{dst0 + j}" for j in range(8)]
                for j in range(8):
                    cop("PE", lambda e, o=p_stat[0:1, 0:512], a=s_invd[:, 0:1],
                        b=pb(src0 + j)[0], f=(j == 0), l=(j == 7):
                        e.matmul(o, a, b, start=f, stop=l),
                        reads=[skeys[j], "k_invd"], writes=["ps_st"])
                cop("ACT", lambda e: e.activation(s_row[0:1, 0:BLK], p_stat[0:1, 0:512], AF.Copy),
                    reads=["ps_st", "k_r0"], writes=["k_r0"])
                cop("PE", lambda e: e.matmul(p_aux0[:, 0:512], s_ones[:],
                                             s_row[0:1, 0:BLK], start=True, stop=True),
                    reads=["k_r0", "k_ones"], writes=["ps_a0"])
                cop("ACT", lambda e: e.activation(s_mub[:], p_aux0[:, 0:512], AF.Copy),
                    reads=["ps_a0", "k_Bb"], writes=["k_Bb"])
                for j in range(8):
                    cop("DVE", lambda e, o=pb(dst0 + j)[0], a=pb(src0 + j)[0], b=s_mub[:]:
                        e.tensor_sub(o, a, b),
                        reads=[skeys[j], "k_Bb", dkeys[j]], writes=[dkeys[j]])
                for j in range(8):
                    sq2, ksq2 = pa(j % 2)
                    cop("ACT", lambda e, o=sq2, i=pb(dst0 + j)[0]:
                        e.activation(o, i, AF.Square),
                        reads=[dkeys[j], ksq2], writes=[ksq2])
                    cop("PE", lambda e, o=p_stat[0:1, 0:512], a=s_invd[:, 0:1], b=sq2,
                        f=(j == 0), l=(j == 7):
                        e.matmul(o, a, b, start=f, stop=l),
                        reads=[ksq2, "k_invd"], writes=["ps_st"])
                cop("ACT", lambda e: e.activation(s_row[0:1, BLK:2 * BLK],
                                                  p_stat[0:1, 0:512], AF.Ln, bias=EPS),
                    reads=["ps_st", "k_r1"], writes=["k_r1"])
                cop("ACT", lambda e: e.activation(s_row[0:1, BLK:2 * BLK],
                                                  s_row[0:1, BLK:2 * BLK], AF.Exp, scale=-0.5),
                    reads=["k_r1"], writes=["k_r1"])
                cop("PE", lambda e: e.matmul(p_aux1[:, 0:512], s_ones[:],
                                             s_row[0:1, BLK:2 * BLK], start=True, stop=True),
                    reads=["k_r1", "k_ones"], writes=["ps_a1"])
                cop("ACT", lambda e: e.activation(s_rsb[:], p_aux1[:, 0:512], AF.Copy),
                    reads=["ps_a1", "k_Cb"], writes=["k_Cb"])
                for j in range(8):
                    cop("DVE", lambda e, o=pb(dst0 + j)[0], a=pb(dst0 + j)[0], b=s_rsb[:]:
                        e.tensor_mul(o, a, b),
                        reads=["k_Cb", dkeys[j]], writes=[dkeys[j]])
                if affine is not None:
                    g_t, b_t, out_fn = affine
                    for j in range(8):
                        o_ap, o_k = out_fn(j)
                        cop("DVE", lambda e, o=o_ap, i=pb(dst0 + j)[0],
                            g=g_t[:, j:j + 1], b2=b_t[:, j:j + 1]:
                            e.tensor_scalar(o, i, g, b2, ALU.mult, ALU.add),
                            reads=[dkeys[j], "k_ln2g", "k_ln2b", o_k], writes=[o_k])

            fm_ln(0, 8, (s_ln2g, s_ln2b, lambda j: pc(j)))
            h2_keys = [f"pC{j}" for j in range(8)]

            for i in range(32):
                kpnA, kpnG = "k_pnl0", "k_pnl1"
                pnA = pnl[:, 0:1024]
                pnG = pnl[:, 1024:2048]
                dma("SP", pnA.rearrange("p (kt c) -> p kt c", kt=8),
                    w_ff1_bf.ap().rearrange("(kt p) c -> p kt c", p=128)[:, :, 128 * i:128 * (i + 1)],
                    reads=["k_wff1bf", kpnA], writes=[kpnA])
                dma("SP", pnG.rearrange("p (kt c) -> p kt c", kt=8),
                    w_ff1_bf.ap().rearrange("(kt p) c -> p kt c", p=128)[:, :, FF + 128 * i:FF + 128 * (i + 1)],
                    reads=["k_wff1bf", kpnG], writes=[kpnG])
                for kt in range(8):
                    cop("PE", lambda e, o=p_big0[:, 0:512], a=pnA[:, kt * 128:(kt + 1) * 128],
                        b=pc(kt)[0], f=(kt == 0), l=(kt == 7):
                        e.matmul(o, a, b, start=f, stop=l),
                        reads=h2_keys + [kpnA], writes=["ps_b0"])
                for kt in range(8):
                    cop("PE", lambda e, o=p_big1[:, 0:512], a=pnG[:, kt * 128:(kt + 1) * 128],
                        b=pc(kt)[0], f=(kt == 0), l=(kt == 7):
                        e.matmul(o, a, b, start=f, stop=l),
                        reads=h2_keys + [kpnG], writes=["ps_b1"])
                sqg, ksqg = pa(0)
                vv2, kvv2 = pa(1)
                tt2, ktt2 = pa(2)
                ga, kga = pa(3)
                gg2, kgg2 = pa(4)
                cop("DVE", lambda e, o=ga, i=p_big0[:, 0:512], b=s_bff1[:, i:i + 1]:
                    e.tensor_scalar(o, i, b, None, ALU.add),
                    reads=["ps_b0", "k_bff1", kga], writes=[kga])
                cop("DVE", lambda e, o=gg2, i=p_big1[:, 0:512], b=s_bff1[:, 32 + i:33 + i]:
                    e.tensor_scalar(o, i, b, None, ALU.add),
                    reads=["ps_b1", "k_bff1", kgg2], writes=[kgg2])
                cop("ACT", lambda e, o=sqg, i=gg2: e.activation(o, i, AF.Square),
                    reads=[kgg2, ksqg], writes=[ksqg])
                cop("POOL", lambda e, o=vv2, i=sqg:
                    e.tensor_scalar(o, i, GC3, 1.0, ALU.mult, ALU.add),
                    reads=[ksqg, kvv2], writes=[kvv2])
                cop("DVE", lambda e, o=tt2, a=vv2, b=gg2: e.tensor_mul(o, a, b),
                    reads=[kvv2, kgg2, ktt2], writes=[ktt2])
                cop("ACT", lambda e, o=tt2, i=tt2: e.activation(o, i, AF.Tanh, scale=GC1),
                    reads=[ktt2], writes=[ktt2])
                cop("DVE", lambda e, o=sqg, i=gg2, t=tt2:
                    e.scalar_tensor_tensor(o, i, 0.5, t, ALU.mult, ALU.mult),
                    reads=[kgg2, ktt2, ksqg], writes=[ksqg])
                cop("DVE", lambda e, o=vv2, i=gg2, t=sqg:
                    e.scalar_tensor_tensor(o, i, 0.5, t, ALU.mult, ALU.add),
                    reads=[kgg2, ksqg, kvv2], writes=[kvv2])
                fi, kfi = pd(i)
                cop("DVE", lambda e, o=fi, a=vv2, b=ga: e.tensor_mul(o, a, b),
                    reads=[kvv2, kga, kfi], writes=[kfi])
            ffin_keys = [f"pD{i}" for i in range(32)]

            for ot in range(8):
                pg, kpg = (p_big0, "ps_b0") if ot % 2 == 0 else (p_big1, "ps_b1")
                for hf in range(4):
                    kpn = f"k_pnl{hf % 2}"
                    pn = pnl[:, (hf % 2) * 1024:(hf % 2 + 1) * 1024]
                    dma("SP", pn.rearrange("p (kt c) -> p kt c", kt=8),
                        w_ff2_bf.ap().rearrange("(kt p) c -> p kt c", p=128)
                        [:, 8 * hf:8 * (hf + 1), 128 * ot:128 * (ot + 1)],
                        reads=["k_wff2bf", kpn], writes=[kpn])
                    for kt in range(8):
                        cop("PE", lambda e, o=pg[:, 0:512], a=pn[:, kt * 128:(kt + 1) * 128],
                            b=pd(8 * hf + kt)[0],
                            f=(hf == 0 and kt == 0), l=(hf == 3 and kt == 7):
                            e.matmul(o, a, b, start=f, stop=l),
                            reads=ffin_keys + [kpn], writes=[kpg])
                xb_, kxb = pb(8 + ot)
                cop("DVE", lambda e, o=xb_, i=pg[:, 0:512], b=s_bff2[:, ot:ot + 1],
                    x=pb(ot)[0]:
                    e.scalar_tensor_tensor(o, i, b, x, ALU.add, ALU.add),
                    reads=[kpg, "k_bff2", f"pB{ot}", kxb], writes=[kxb])

            for j in range(8):
                s1, ks1 = pb(j)
                cop("DVE", lambda e, o=s1, a=pb(8 + j)[0],
                    b=s_xfm[:, j * BLK:(j + 1) * BLK]:
                    e.tensor_add(o, a, b),
                    reads=[f"pB{8 + j}", f"k_xfm{j}", ks1], writes=[ks1])
            fm_ln(0, 8, None)
            for j in range(8):
                xcb, kxcb = pc(8 + j)
                cop("POOL", lambda e, o=xcb, i=pb(8 + j)[0]: e.tensor_copy(o, i),
                    reads=[f"pB{8 + j}", kxcb], writes=[kxcb])
            xcb_keys = [f"pC{8 + j}" for j in range(8)]

            for q in range(4):
                pg, kpg = (p_big0, "ps_b0") if q % 2 == 0 else (p_big1, "ps_b1")
                for kt in range(4):
                    cop("PE", lambda e, o=pg[:, 0:512],
                        a=s_wrt[:, kt * 512 + 128 * q:kt * 512 + 128 * q + 128],
                        b=pc(8 + kt)[0], f=(kt == 0), l=False:
                        e.matmul(o, a, b, start=f, stop=l),
                        reads=xcb_keys + ["k_wrt"], writes=[kpg])
                for kt in range(4):
                    cop("PE", lambda e, o=pg[:, 0:512],
                        a=s_witn[:, kt * 512 + 128 * q:kt * 512 + 128 * q + 128],
                        b=pc(12 + kt)[0], f=False, l=(kt == 3):
                        e.matmul(o, a, b, start=f, stop=l),
                        reads=xcb_keys + ["k_witn"], writes=[kpg])
                s2, ks2 = pb(q)
                cop("DVE", lambda e, o=s2, i=pg[:, 0:512], b=s_br[:, q:q + 1],
                    x=pb(8 + q)[0]:
                    e.scalar_tensor_tensor(o, i, b, x, ALU.add, ALU.add),
                    reads=[kpg, "k_br", f"pB{8 + q}", ks2], writes=[ks2])
            for q in range(4):
                pg, kpg = (p_big0, "ps_b0") if q % 2 == 0 else (p_big1, "ps_b1")
                for kt in range(4):
                    cop("PE", lambda e, o=pg[:, 0:512],
                        a=s_wit[:, kt * 512 + 128 * q:kt * 512 + 128 * q + 128],
                        b=pc(8 + kt)[0], f=(kt == 0), l=False:
                        e.matmul(o, a, b, start=f, stop=l),
                        reads=xcb_keys + ["k_wit"], writes=[kpg])
                for kt in range(4):
                    cop("PE", lambda e, o=pg[:, 0:512],
                        a=s_wrt[:, kt * 512 + 128 * q:kt * 512 + 128 * q + 128],
                        b=pc(12 + kt)[0], f=False, l=(kt == 3):
                        e.matmul(o, a, b, start=f, stop=l),
                        reads=xcb_keys + ["k_wrt"], writes=[kpg])
                s2, ks2 = pb(4 + q)
                cop("DVE", lambda e, o=s2, i=pg[:, 0:512], b=s_bi[:, q:q + 1],
                    x=pb(12 + q)[0]:
                    e.scalar_tensor_tensor(o, i, b, x, ALU.add, ALU.add),
                    reads=[kpg, "k_bi", f"pB{12 + q}", ks2], writes=[ks2])

            fm_ln(0, 8, None)

            for tk in range(4):
                for half in range(2):
                    for jj4 in range(4):
                        jj = half * 4 + jj4
                        cop("PE", lambda e, o=p_trf[:, 128 * jj4:128 * (jj4 + 1)],
                            i=pb(8 + jj)[0][:, 128 * tk:128 * (tk + 1)], idn=s_idf[:]:
                            e.transpose(o, i, idn),
                            reads=[f"pB{8 + jj}", "k_idf", "ps_tf"], writes=["ps_tf"])
                    for jj4 in range(4):
                        jj = half * 4 + jj4
                        cop("ACT", lambda e, o=s_xt[:, 128 * jj:128 * (jj + 1)],
                            i=p_trf[:, 128 * jj4:128 * (jj4 + 1)]:
                            e.activation(o, i, AF.Copy),
                            reads=["ps_tf", "k_xt"], writes=["k_xt"])
                dma("SP", out_d[BLK * ph + 128 * tk:BLK * ph + 128 * (tk + 1), :],
                    s_xt[:], reads=["k_xt"], writes=["d_out"])

        P.emit("SP", "fence", None, reads=["d_out"])

        sec = {"SP": block.sync, "PE": block.tensor, "ACT": block.scalar,
               "DVE": block.vector, "POOL": block.gpsimd}

        def make_body(name):
            def body(e):
                for waits, fn, inc in P.entries[name]:
                    for sem, val in waits:
                        e.wait_ge(sems[sem], val)
                    if fn is not None:
                        inst = fn(e)
                        if inc is not None:
                            s, v = inc
                            if v is None:
                                inst.then_inc(sems[s])
                            else:
                                inst.then_inc(sems[s], v)
            return body

        for name in ENGINES:
            sec[name](make_body(name))

    return nc


_NC_CACHE = [None]


def kernel(**inputs):
    f32 = np.float32
    x = np.asarray(inputs["L0_attn_x"], f32)[0]
    Ltot = x.shape[0]

    def t(name):
        return np.ascontiguousarray(np.asarray(inputs[name], f32))

    ln1_g, ln1_b = t("L0_attn_ln1_g"), t("L0_attn_ln1_b")
    w_in = t("L0_attn_w_in")
    conv_w, conv_b = t("L0_attn_conv_w"), t("L0_attn_conv_b")
    w_x, w_dt, b_dt = t("L0_attn_w_x"), t("L0_attn_w_dt"), t("L0_attn_b_dt")
    a_log, d_skip = t("L0_attn_a_log"), t("L0_attn_d_skip")
    w_out = t("L0_attn_w_out")
    ln2_g, ln2_b = t("L0_attn_ln2_g"), t("L0_attn_ln2_b")
    w_ff1, b_ff1 = t("L0_attn_w_ff1"), t("L0_attn_b_ff1")
    w_ff2, b_ff2 = t("L0_attn_w_ff2"), t("L0_attn_b_ff2")
    G1R, G2R = t("L0_ffn_G1_R"), t("L0_ffn_G2_R")
    G1I, G2I = t("L0_ffn_G1_I"), t("L0_ffn_G2_I")
    br, bi = t("L0_ffn_bias_real"), t("L0_ffn_bias_imag")

    shared = dict(
        ln1g_b=np.ascontiguousarray(np.tile(ln1_g[None, :], (128, 1))),
        ln1b_b=np.ascontiguousarray(np.tile(ln1_b[None, :], (128, 1))),
        w_in_f=w_in,
        conv_w_fm=np.ascontiguousarray(
            conv_w.reshape(16, 128, 4).transpose(1, 0, 2).reshape(128, 64)),
        conv_b_fm=np.ascontiguousarray(conv_b.reshape(16, 128).T),
        w_x_fm=np.ascontiguousarray(
            w_x.reshape(16, 128, 96).transpose(1, 0, 2).reshape(128, 16 * 96)),
        w_dt_f=w_dt,
        b_dt_fm=np.ascontiguousarray(b_dt.reshape(16, 128).T),
        d_skip_fm=np.ascontiguousarray(d_skip.reshape(16, 128).T),
        w_out_f=w_out,
        ln2g_fm=np.ascontiguousarray(ln2_g.reshape(8, 128).T),
        ln2b_fm=np.ascontiguousarray(ln2_b.reshape(8, 128).T),
        w_ff1_f=w_ff1,
        b_ff1_fm=np.ascontiguousarray(b_ff1.reshape(64, 128).T),
        w_ff2_f=w_ff2,
        b_ff2_fm=np.ascontiguousarray(b_ff2.reshape(8, 128).T),
        g1_in=np.ascontiguousarray(np.stack([G1R[0], G1I[0]])),
        g2_perm=np.ascontiguousarray(
            np.stack([G2R[:, :, 0, :].transpose(0, 2, 1),
                      G2I[:, :, 0, :].transpose(0, 2, 1)])),
        br_fm=np.ascontiguousarray(br.reshape(4, 128).T),
        bi_fm=np.ascontiguousarray(bi.reshape(4, 128).T),
        idf=np.eye(128, dtype=f32),
        idb=np.eye(128).astype(ml_dtypes.bfloat16),
        ones_r=np.ones((1, 128), f32),
        half_r=np.full((1, 128), 0.5, f32),
        invd_c=np.full((128, 1), 1.0 / D, f32),
        consts_in=np.tile(np.array([1e-5, 1.0 / D, 0.5, -0.5, -1.0, 1.0,
                                    0.044715, 0.7978845608028654, 2.0, 0.0,
                                    0.25, -2.0, 4.0, 0.125, 8.0, 3.0],
                                   f32)[None, :], (128, 1)),
    )

    in_maps = []
    for core in range(NC):
        xbv = np.zeros((NPH, 3 + BLK, D), f32)
        hm = np.ones((128, NPH), f32)
        for c in range(NPH):
            g = 8 * c + core
            lo = g * BLK
            xbv[c, 3:, :] = x[lo:lo + BLK]
            if g > 0:
                xbv[c, 0:3, :] = x[lo - 3:lo]
            else:
                hm[:, c] = 0.0
        m = dict(shared)
        m["xb"] = xbv
        m["hmask"] = hm
        m["a_log_my"] = np.ascontiguousarray(
            a_log[256 * core:256 * (core + 1)].reshape(2, 128, DS))
        in_maps.append(m)

    if _NC_CACHE[0] is None:
        _NC_CACHE[0] = build_program()
    nc = _NC_CACHE[0]
    res = run_bass_kernel_spmd(nc, in_maps, core_ids=list(range(NC)))

    out = np.zeros((Ltot, D), f32)
    for core in range(NC):
        o = res.results[core]["out"]
        for c in range(NPH):
            g = 8 * c + core
            out[g * BLK:(g + 1) * BLK] = o[c * BLK:(c + 1) * BLK]
    return out.reshape(1, Ltot, D)


if __name__ == "__main__":
    import sys
    d = np.load(sys.argv[1])
    ins = {k: d[k] for k in d.files}
    got = kernel(**ins)
    exp = np.load(sys.argv[2])["exp_0"]
    err = got.astype(np.float64) - exp.astype(np.float64)
    rel = float(np.sqrt((err ** 2).mean() / (exp.astype(np.float64) ** 2).mean()))
    print("rel_l2:", rel, "maxabs:", float(np.abs(err).max()))
